# version 01
# baseline (speedup 1.0000x reference)
import os
from contextlib import ExitStack

import numpy as np
import ml_dtypes

import concourse.bass as bass
import concourse.mybir as mybir
from concourse.bass_utils import run_bass_kernel_spmd

F32 = mybir.dt.float32
BF16 = mybir.dt.bfloat16
AF = mybir.ActivationFunctionType
ALU = mybir.AluOpType
AX = mybir.AxisListType

BB, S, D = 4, 8192, 1024
HALF = 4096
NT_S = S // 128
NT_H = HALF // 128
PADT = 10
NT_L = PADT + NT_H
L_AQ0 = (4, 8)
L_AK0 = (0, 4)
GT = PADT
WIN = 512
HEADS, DH = 8, 64
INNER = 512
HID = 2730
LN3 = 1.0986122886681098
SCL_A = LN3 / 32.0
NEG = -3.0e38

_DBG = os.environ.get("MK_DBG", "")
_SERIAL = bool(os.environ.get("MK_SERIAL", ""))
_PHASES = os.environ.get("MK_PHASES", "0,A,B0,B1,F").split(",")


def _host_tables(h):
    QS = HALF * h
    t = np.arange(S, dtype=np.float32)
    inv = (1.0 / 10000.0 ** (np.arange(512, dtype=np.float32) / 512.0)).astype(np.float32)
    ang = t[:, None] * inv[None, :]
    cosr = np.roll(np.round(np.cos(ang)), -QS, axis=0).astype(ml_dtypes.bfloat16)
    sinr = np.roll(np.round(np.sin(ang)), -QS, axis=0).astype(ml_dtypes.bfloat16)
    half_idx = np.arange(0, DH, 2, dtype=np.float32)
    invf = (1.0 / 10000.0 ** (half_idx / DH)).astype(np.float32)
    freqs = t[:, None] * invf[None, :]
    base = ((half_idx + 0.4 * DH) / (1.4 * DH)).astype(np.float32)
    power = ((t - S // 2) / float(WIN // 2)).astype(np.float32)
    scale = (base[None, :] ** power[:, None]).astype(np.float32)
    cosx = np.cos(freqs).astype(np.float32)
    sinx = np.sin(freqs).astype(np.float32)
    gp = (QS + np.arange(NT_L * 128) - PADT * 128) % S
    xposP = np.concatenate(
        [cosx[gp] * scale[gp], sinx[gp] * scale[gp],
         cosx[gp] / scale[gp], sinx[gp] / scale[gp]], axis=1
    ).astype(np.float32)
    r = np.arange(128)[:, None]
    c = np.arange(640)[None, :]
    band = (c >= r) & (c <= r + WIN)
    masks = np.full((5, 128, 640), NEG, dtype=np.float32)
    for k in range(4):
        g0 = 128 * k
        ok = (band & (c >= WIN - g0)) if h == 0 else band
        masks[k][ok] = 0.0
    masks[4][band] = 0.0
    return dict(
        cosr=cosr, sinr=sinr, xposP=xposP,
        masksB=np.ascontiguousarray(masks.astype(ml_dtypes.bfloat16)),
        gate=np.full((128, 1), 0.0 if h == 0 else 1.0, dtype=np.float32),
        shiftm=np.eye(128, k=1, dtype=np.float32),
        identb=np.eye(128).astype(ml_dtypes.bfloat16),
    )


_TABLES = None


def _tables():
    global _TABLES
    if _TABLES is None:
        _TABLES = [_host_tables(0), _host_tables(1)]
    return _TABLES


class Ctx:

    def __init__(self, nc, stack):
        self.nc = nc
        self.stack = stack
        self._sems = {}
        self.cnt = {}
        for name in ("LD", "LG", "ST", "TC", "VC", "SC", "GC"):
            self._sems[name] = stack.enter_context(nc.semaphore(name))
            self.cnt[name] = 0

    def inc(self, ins, name, k=1):
        ins.then_inc(self._sems[name], k)
        self.cnt[name] += k
        return self.cnt[name]

    def w(self, eng, name, v=None):
        if v is None:
            v = self.cnt[name]
        if v > 0:
            eng.wait_ge(self._sems[name], v)

    def vs(self, eng, ins, name="VC"):
        self.inc(ins, name)
        self.w(eng, name)


def build_nc(dbg=""):
    nc = bass.Bass("TRN2", target_bir_lowering=False)
    ap = {}

    def din(name, shape, dt=F32):
        ap[name] = nc.dram_tensor(name, shape, dt, kind="ExternalInput").ap()

    def dout(name, shape, dt=F32):
        ap[name] = nc.dram_tensor(name, shape, dt, kind="ExternalOutput").ap()

    def dscr(name, shape, dt):
        kind = "ExternalOutput" if name in dbg else "Internal"
        ap[name] = nc.dram_tensor(name, shape, dt, kind=kind).ap()

    din("x0r", [S, D])
    din("wqkv_a", [D, 3 * D]); din("bqkv_a", [3 * D]); din("gnorm", [D])
    din("wproj", [D, D]); din("bproj", [D])
    din("cosr", [S, 512], BF16); din("sinr", [S, 512], BF16)
    din("wqkv_b", [2, D, 3 * INNER]); din("qscale", [2, DH]); din("kscale", [2, DH])
    din("wo_b", [2, INNER, D]); din("lng", [2, D]); din("lnb", [2, D])
    din("w1_b", [2, D, 2 * HID]); din("b1_b", [2, 2 * HID])
    din("w2_b", [2, HID, D]); din("b2_b", [2, D])
    din("xposP", [NT_L * 128, 128]); din("masksB", [5, 128, 640], BF16)
    din("gate", [128, 1]); din("shiftm", [128, 128]); din("identb", [128, 128], BF16)
    dout("out", [HALF, D])

    dscr("XN", [S, D], F32)
    dscr("QFM", [8, 128, HALF], BF16)
    dscr("KFM", [8, 128, S], BF16)
    dscr("VTM", [S, D], BF16)
    dscr("SOUT", [HALF, D], F32)
    dscr("XMID", [NT_L * 128, D], F32)
    dscr("X1", [NT_L * 128, D], F32)
    dscr("XMID2", [NT_L * 128, D], F32)
    dscr("X2", [NT_L * 128, D], F32)

    with ExitStack() as stack:
        cx = Ctx(nc, stack)
        cx.ap = ap
        cx.identb = stack.enter_context(nc.sbuf_tensor("g_identb", [128, 128], BF16))
        cx.shiftm = stack.enter_context(nc.sbuf_tensor("g_shiftm", [128, 128], F32))
        cx.gate = stack.enter_context(nc.sbuf_tensor("g_gate", [128, 1], F32))
        with nc.Block() as blk:
            @blk.sync
            def _(e):
                cx.inc(e.dma_start(out=cx.identb[:], in_=ap["identb"]), "LD", 16)
                cx.inc(e.dma_start(out=cx.shiftm[:], in_=ap["shiftm"]), "LD", 16)
                cx.inc(e.dma_start(out=cx.gate[:], in_=ap["gate"]), "LD", 16)

        if "0" in _PHASES:
            phase0(cx)
        if "A" in _PHASES:
            phaseA(cx)
        if "B0" in _PHASES:
            phaseB(cx, 0)
        if "B1" in _PHASES:
            phaseB(cx, 1)
        if "F" in _PHASES:
            phaseF(cx)

        with nc.Block() as blk:
            @blk.sync
            def _(e):
                cx.w(e, "ST")
                cx.w(e, "LG")
    return nc


def phase0(cx):
    nc, ap, c = cx.nc, cx.ap, cx
    with ExitStack() as st:
        def sb(name, shape, dt):
            return st.enter_context(nc.sbuf_tensor(name, shape, dt))

        def psum(name, shape, dt=F32):
            return st.enter_context(nc.psum_tensor(name, shape, dt))

        wq = sb("p0_wq", [128, 8, 3072], BF16)
        wtmp = sb("p0_wtmp", [128, 3072], F32)
        gfm = sb("p0_gfm", [128, 8], F32)
        bqr = sb("p0_bqr", [128, 3072], F32)
        xin = [sb(f"p0_x{i}", [128, 1024], F32) for i in range(2)]
        ctab = [sb(f"p0_ct{i}", [128, 512], BF16) for i in range(2)]
        stab = [sb(f"p0_st{i}", [128, 512], BF16) for i in range(2)]
        xn = sb("p0_xn", [128, 1024], F32)
        ha = sb("p0_ha", [128, 1024], BF16)
        hT = sb("p0_hT", [128, 8, 128], BF16)
        qb = sb("p0_qb", [128, 1024], F32)
        kb = sb("p0_kb", [128, 1024], F32)
        vb = sb("p0_vb", [128, 1024], BF16)
        t1 = sb("p0_t1", [128, 1024], F32)
        t2 = sb("p0_t2", [128, 1024], F32)
        qrb = sb("p0_qrb", [128, 1024], BF16)
        krb = sb("p0_krb", [128, 1024], BF16)
        qT = sb("p0_qT", [128, 8, 128], BF16)
        kT = sb("p0_kT", [128, 8, 128], BF16)
        sA = sb("p0_sA", [128, 1], F32)
        sQ = sb("p0_sQ", [128, 1], F32)
        sqd = sb("p0_sqd", [128, 1024], F32)
        m_ = sb("p0_m", [128, 1], F32)
        va = sb("p0_va", [128, 1], F32)
        tt_ = sb("p0_tt", [128, 1], F32)
        lnv = sb("p0_lnv", [128, 1], F32)
        inv1 = sb("p0_inv1", [128, 1], F32)
        r1 = sb("p0_r1", [128, 1], F32)
        inv3 = sb("p0_inv3", [128, 1], F32)
        eps5 = sb("p0_eps5", [128, 1], F32)
        eps6 = sb("p0_eps6", [128, 1], F32)
        ps_q = psum("p0_psq", [128, 1024])
        ps_k = psum("p0_psk", [128, 1024])
        ps_v = psum("p0_psv", [128, 1024])
        ps_t1 = psum("p0_pst1", [128, 8, 128], BF16)
        ps_t2 = psum("p0_pst2", [128, 8, 128], BF16)

        vcW = c.cnt["VC"]
        lgW = c.cnt["LG"]
        with nc.Block() as blk:
            @blk.sync
            def _(e):
                c.inc(e.dma_start(out=gfm[:], in_=ap["gnorm"].rearrange("(c p) -> p c", p=128), allow_slow_non_contiguous=True), "LD", 16)
                c.inc(e.dma_start(out=bqr[:], in_=ap["bqkv_a"][None, :].broadcast_to([128, 3072])), "LD", 16)
            ld0 = c.cnt["LD"]

            @blk.gpsimd
            def _(e):
                for ch in range(8):
                    if ch >= 1:
                        e.wait_ge(c._sems["VC"], vcW + ch)
                    e.dma_start(out=wtmp[:], in_=ap["wqkv_a"][128 * ch:128 * (ch + 1), :]).then_inc(
                        c._sems["LG"], 16)
            c.cnt["LG"] += 16 * 8

            @blk.vector
            def _(e):
                e.memset(eps5[:], 1e-5)
                e.memset(eps6[:], 1e-6)
                c.w(e, "LD", ld0)
                e.tensor_scalar(out=bqr[:, :1024], in0=bqr[:, :1024], scalar1=float(SCL_A),
                                scalar2=None, op0=ALU.mult)
                for ch in range(8):
                    e.wait_ge(c._sems["LG"], lgW + 16 * (ch + 1))
                    e.tensor_scalar(out=wq[:, ch, :], in0=wtmp[:], scalar1=gfm[:, ch:ch + 1],
                                    scalar2=None, op0=ALU.mult)
                    ins = e.tensor_scalar(out=wq[:, ch, :1024], in0=wq[:, ch, :1024],
                                          scalar1=float(SCL_A), scalar2=None, op0=ALU.mult)
                    ins.then_inc(c._sems["VC"], 1)
            c.cnt["VC"] += 8
        w_ready = c.cnt["VC"]

        ld_tok = [None] * (NT_S + 2)
        free_tok = [0] * NT_S

        def issue(e, i):
            if i >= NT_S:
                return
            bb = i % 2
            if i >= 2:
                c.w(e, "VC", free_tok[i - 2])
            c.inc(e.dma_start(out=xin[bb][:], in_=ap["x0r"][128 * i:128 * (i + 1), :]), "LD", 16)
            c.inc(e.dma_start(out=ctab[bb][:], in_=ap["cosr"][128 * i:128 * (i + 1), :]), "LD", 16)
            c.inc(e.dma_start(out=stab[bb][:], in_=ap["sinr"][128 * i:128 * (i + 1), :]), "LD", 16)
            ld_tok[i] = c.cnt["LD"]

        evac_tok = 0
        hT_free = 0
        t2_free = 0
        var_prev = 0
        xn_st_prev = 0
        tr1_prev = 0
        for i in range(NT_S):
            b = i % 2
            own = i < NT_H
            with nc.Block() as blk:
                if i == 0:
                    @blk.sync
                    def _(e):
                        issue(e, 0)
                        issue(e, 1)

                @blk.scalar
                def _(e):
                    c.w(e, "LD", ld_tok[i])
                    c.w(e, "VC", var_prev)
                    c.inc(e.activation(out=sqd[:], in_=xin[b][:], func=AF.Square,
                                       accum_out=sQ[:]), "SC")
                sq_tok = c.cnt["SC"]

                @blk.vector
                def _(e):
                    c.w(e, "LD", ld_tok[i])
                    c.vs(e, e.reduce_sum(out=sA[:], in_=xin[b][:], axis=AX.X))
                    c.vs(e, e.tensor_scalar(out=m_[:], in0=sA[:], scalar1=1.0 / 1024,
                                            scalar2=None, op0=ALU.mult))
                    c.w(e, "SC", sq_tok)
                    c.vs(e, e.tensor_scalar(out=va[:], in0=sQ[:], scalar1=1.0 / 1024,
                                            scalar2=None, op0=ALU.mult))
                    c.vs(e, e.tensor_tensor(out=tt_[:], in0=m_[:], in1=m_[:], op=ALU.mult))
                    c.inc(e.tensor_tensor(out=va[:], in0=va[:], in1=tt_[:], op=ALU.subtract), "VC")
                var_tok = c.cnt["VC"]
                var_prev = var_tok

                @blk.scalar
                def _(e):
                    c.w(e, "VC", var_tok)
                    e.activation(out=lnv[:], in_=va[:], func=AF.Ln, bias=eps5[:])
                    c.inc(e.activation(out=inv1[:], in_=lnv[:], func=AF.Exp, scale=-0.5), "SC")
                inv1_tok = c.cnt["SC"]

                @blk.vector
                def _(e):
                    c.w(e, "SC", inv1_tok)
                    c.vs(e, e.tensor_tensor(out=tt_[:], in0=inv1[:], in1=inv1[:], op=ALU.mult))
                    c.inc(e.tensor_tensor(out=r1[:], in0=va[:], in1=tt_[:], op=ALU.mult), "VC")
                r1_tok = c.cnt["VC"]

                @blk.scalar
                def _(e):
                    c.w(e, "VC", r1_tok)
                    e.activation(out=lnv[:], in_=r1[:], func=AF.Ln, bias=eps6[:])
                    c.inc(e.activation(out=r1[:], in_=lnv[:], func=AF.Exp, scale=-0.5), "SC")
                r2_tok = c.cnt["SC"]

                @blk.vector
                def _(e):
                    c.w(e, "SC", r2_tok)
                    c.inc(e.tensor_tensor(out=inv3[:], in0=inv1[:], in1=r1[:], op=ALU.mult), "VC")
                    c.w(e, "VC")
                    c.w(e, "ST", xn_st_prev)
                    c.w(e, "TC", tr1_prev)
                    e.tensor_scalar(out=xn[:], in0=xin[b][:], scalar1=m_[:], scalar2=inv1[:],
                                    op0=ALU.subtract, op1=ALU.mult)
                    c.inc(e.tensor_scalar(out=ha[:], in0=xin[b][:], scalar1=m_[:], scalar2=inv3[:],
                                          op0=ALU.subtract, op1=ALU.mult), "VC")
                xn_tok = c.cnt["VC"]

                @blk.sync
                def _(e):
                    c.w(e, "VC", xn_tok)
                    c.inc(e.dma_start(out=ap["XN"][128 * i:128 * (i + 1), :], in_=xn[:]), "ST", 16)
                xn_st_prev = c.cnt["ST"]

                @blk.tensor
                def _(e):
                    c.w(e, "VC", xn_tok)
                    c.w(e, "SC", hT_free)
                    for ch in range(8):
                        ins = e.transpose(out=ps_t1[:, ch, :], in_=ha[:, 128 * ch:128 * (ch + 1)],
                                          identity=cx.identb[:])
                    c.inc(ins, "TC")
                tr1_prev = c.cnt["TC"]

                @blk.scalar
                def _(e):
                    c.w(e, "TC")
                    c.inc(e.copy(out=hT[:], in_=ps_t1[:]), "SC")
                hT_free = c.cnt["SC"]

                @blk.tensor
                def _(e):
                    c.w(e, "SC", hT_free)
                    if i == 0:
                        c.w(e, "VC", w_ready)
                    c.w(e, "VC", evac_tok)
                    cols = ([(ps_q, 0)] if own else []) + [(ps_k, 1024), (ps_v, 2048)]
                    for (pst, c0) in cols:
                        for hf in range(2):
                            for ch in range(8):
                                ins = e.matmul(
                                    out=pst[:, 512 * hf:512 * (hf + 1)],
                                    lhsT=hT[:, ch, :],
                                    rhs=wq[:, ch, c0 + 512 * hf:c0 + 512 * (hf + 1)],
                                    start=(ch == 0), stop=(ch == 7))
                    c.inc(ins, "TC")
                mm_tok = c.cnt["TC"]

                @blk.vector
                def _(e):
                    c.w(e, "TC", mm_tok)
                    if own:
                        e.tensor_tensor(out=qb[:], in0=ps_q[:], in1=bqr[:, :1024], op=ALU.add)
                    e.tensor_tensor(out=kb[:], in0=ps_k[:], in1=bqr[:, 1024:2048], op=ALU.add)
                    c.inc(e.tensor_tensor(out=vb[:], in0=ps_v[:], in1=bqr[:, 2048:3072],
                                          op=ALU.add), "VC")
                    evac_l = c.cnt["VC"]
                    def rope(src, dstb):
                        e.tensor_tensor(out=t1[:].rearrange("p (r x) -> p r x", r=2),
                                        in0=src[:].rearrange("p (r x) -> p r x", r=2),
                                        in1=ctab[b][:, None, :].broadcast_to([128, 2, 512]),
                                        op=ALU.mult)
                        e.tensor_tensor(out=t2[:, :512], in0=src[:, 512:], in1=stab[b][:],
                                        op=ALU.mult)
                        e.tensor_tensor(out=t2[:, 512:], in0=src[:, :512], in1=stab[b][:],
                                        op=ALU.mult)
                        e.tensor_tensor(out=dstb[:, :512], in0=t1[:, :512], in1=t2[:, :512],
                                        op=ALU.subtract)
                        return e.tensor_tensor(out=dstb[:, 512:], in0=t1[:, 512:],
                                               in1=t2[:, 512:], op=ALU.add)
                    ins = rope(kb, krb)
                    if own:
                        ins = rope(qb, qrb)
                    c.inc(ins, "VC")
                evac_tok = c.cnt["VC"] - 1
                free_tok[i] = c.cnt["VC"]
                rope_tok = c.cnt["VC"]

                @blk.sync
                def _(e):
                    issue(e, i + 2)

                @blk.tensor
                def _(e):
                    c.w(e, "VC", rope_tok)
                    c.w(e, "SC", t2_free)
                    for ch in range(8):
                        ins = e.transpose(out=ps_t2[:, ch, :], in_=krb[:, 128 * ch:128 * (ch + 1)],
                                          identity=cx.identb[:])
                    c.inc(ins, "TC")

                @blk.scalar
                def _(e):
                    c.w(e, "TC")
                    c.inc(e.copy(out=kT[:], in_=ps_t2[:]), "SC")
                kT_tok = c.cnt["SC"]
                t2_free = kT_tok

                if own:
                    @blk.tensor
                    def _(e):
                        c.w(e, "SC", kT_tok)
                        for ch in range(8):
                            ins = e.transpose(out=ps_t2[:, ch, :],
                                              in_=qrb[:, 128 * ch:128 * (ch + 1)],
                                              identity=cx.identb[:])
                        c.inc(ins, "TC")

                    @blk.scalar
                    def _(e):
                        c.w(e, "TC")
                        c.inc(e.copy(out=qT[:], in_=ps_t2[:]), "SC")
                    t2_free = c.cnt["SC"]

                out_tok = c.cnt["SC"]

                @blk.gpsimd
                def _(e):
                    c.w(e, "SC", out_tok)
                    c.w(e, "VC", rope_tok)
                    c.inc(e.dma_start(out=ap["KFM"][:, :, 128 * i:128 * (i + 1)].rearrange("c p t -> p c t"),
                                      in_=kT[:]), "LG", 16)
                    c.inc(e.dma_start(out=ap["VTM"][128 * i:128 * (i + 1), :], in_=vb[:]),
                          "LG", 16)
                    if own:
                        c.inc(e.dma_start(out=ap["QFM"][:, :, 128 * i:128 * (i + 1)].rearrange("c p t -> p c t"),
                                          in_=qT[:]), "LG", 16)
                lg_now = c.cnt["LG"]
                free_tok[i] = max(free_tok[i], rope_tok)
            if i + 1 < NT_S:
                with nc.Block() as blk2:
                    @blk2.vector
                    def _(e):
                        c.w(e, "LG", lg_now)

                    @blk2.scalar
                    def _(e):
                        c.w(e, "LG", lg_now)
                    if _SERIAL:
                        for eng in (blk2.vector, blk2.scalar, blk2.tensor,
                                    blk2.gpsimd, blk2.sync):
                            def _barrier(e):
                                for s in ("LD", "LG", "ST", "TC", "VC", "SC"):
                                    c.w(e, s)
                            eng(_barrier)

        with nc.Block() as blk:
            @blk.sync
            def _(e):
                c.w(e, "LG")
                c.w(e, "ST")



def phaseA(cx):
    nc, ap, c = cx.nc, cx.ap, cx
    with ExitStack() as st:
        def sb(name, shape, dt):
            return st.enter_context(nc.sbuf_tensor(name, shape, dt))

        def psum(name, shape, dt=F32):
            return st.enter_context(nc.psum_tensor(name, shape, dt))

        wp = sb("pa_wp", [128, 8, 1024], BF16)
        bpr = sb("pa_bpr", [128, 1024], F32)
        qf = [sb(f"pa_qf{j}", [128, 8, 128], BF16) for j in range(8)]
        O = [sb(f"pa_O{j}", [128, 1024], F32) for j in range(8)]
        kblk = [sb(f"pa_k{h}", [128, 8, 1024], BF16) for h in range(2)]
        vblk = [sb(f"pa_v{h}", [128, 8, 1024], BF16) for h in range(2)]
        S_sb = sb("pa_S", [128, 2048], F32)
        P_sb = sb("pa_P", [128, 2048], BF16)
        at_sb = sb("pa_at", [128, 16, 128], BF16)
        nm = sb("pa_nm", [128, 8, 2], F32)
        ll = sb("pa_l", [128, 8], F32)
        nmb = sb("pa_nmb", [128, 1], F32)
        dd = sb("pa_dd", [128, 1], F32)
        corr = sb("pa_corr", [128, 1], F32)
        lb = sb("pa_lb", [128, 1], F32)
        linv = sb("pa_linv", [128, 8], F32)
        ob = sb("pa_ob", [128, 1024], BF16)
        oT = sb("pa_oT", [128, 8, 128], BF16)
        xnt = [sb(f"pa_xn{h}", [128, 1024], F32) for h in range(2)]
        stile = [sb(f"pa_s{h}", [128, 1024], F32) for h in range(2)]
        ps_s = [psum(f"pa_pss{h}", [128, 1024]) for h in range(2)]
        ps_at = psum("pa_psat", [128, 16, 128], BF16)
        ps_pv = psum("pa_pspv", [128, 1024])

        with nc.Block() as blk:
            @blk.gpsimd
            def _(e):
                c.w(e, "LG")
                c.inc(e.dma_start(out=wp[:], in_=ap["wproj"].rearrange(
                    "(c p) n -> p c n", p=128)), "LG", 16)

            @blk.sync
            def _(e):
                c.w(e, "ST")
                c.inc(e.dma_start(out=bpr[:], in_=ap["bproj"][None, :].broadcast_to(
                    [128, 1024])), "LD", 16)
        wp_lg, wp_ld = c.cnt["LG"], c.cnt["LD"]

        state = {"pend": None, "p_tok": 0, "at_evac": 0, "pv_done": 0,
                 "sc_evac_v": 0, "sc_evac_s": 0}

        def flush_pending():
            pj = state["pend"]
            if pj is None:
                return
            with nc.Block() as blk:
                @blk.tensor
                def _(e):
                    c.w(e, "SC", state["p_tok"])
                    c.w(e, "SC", state["at_evac"])
                    for ch2 in range(16):
                        ins = e.transpose(out=ps_at[:, ch2, :],
                                          in_=P_sb[:, 128 * ch2:128 * (ch2 + 1)],
                                          identity=cx.identb[:])
                    c.inc(ins, "TC")
                tr_tok = c.cnt["TC"]

                @blk.scalar
                def _(e):
                    c.w(e, "TC", tr_tok)
                    c.inc(e.copy(out=at_sb[:], in_=ps_at[:]), "SC")
                state["at_evac"] = c.cnt["SC"]

                @blk.tensor
                def _(e):
                    c.w(e, "SC", state["at_evac"])
                    c.w(e, "VC", state["pv_done"])
                    for h2 in range(2):
                        for ch2 in range(16):
                            ins = e.matmul(out=ps_pv[:, 512 * h2:512 * (h2 + 1)],
                                           lhsT=at_sb[:, ch2, :],
                                           rhs=vblk[ch2 // 8][:, ch2 % 8,
                                                             512 * h2:512 * (h2 + 1)],
                                           start=(ch2 == 0), stop=(ch2 == 15))
                    c.inc(ins, "TC")
                pvmm = c.cnt["TC"]

                @blk.vector
                def _(e):
                    c.w(e, "TC", pvmm)
                    c.inc(e.tensor_tensor(out=O[pj][:], in0=O[pj][:], in1=ps_pv[:],
                                          op=ALU.add), "VC")
                state["pv_done"] = c.cnt["VC"]
            state["pend"] = None

        kv_free = (0, 0)
        for g in range(4):
            with nc.Block() as blk:
                @blk.sync
                def _(e):
                    if g == 0:
                        c.w(e, "LG", wp_lg)
                    else:
                        c.w(e, "TC")
                    for j in range(8):
                        jj = 8 * g + j
                        c.inc(e.dma_start(out=qf[j][:], in_=ap["QFM"][
                            :, :, 128 * jj:128 * (jj + 1)].rearrange("c p t -> p c t")),
                            "LD", 16)

                @blk.vector
                def _(e):
                    c.w(e, "VC")
                    for j in range(8):
                        e.memset(O[j][:], 0.0)
                    e.memset(nm[:, :, 0:1], 3.0e38)
                    c.inc(e.memset(ll[:], 0.0), "VC")
            qf_tok = c.cnt["LD"]
            init_tok = c.cnt["VC"]

            for sbi in range(4):
                kb0 = 2048 * sbi
                with nc.Block() as blk:
                    @blk.sync
                    def _(e):
                        c.w(e, "TC", kv_free[0])
                        c.w(e, "VC", kv_free[1])
                        for h in range(2):
                            c.inc(e.dma_start(out=kblk[h][:], in_=ap["KFM"][
                                :, :, kb0 + 1024 * h:kb0 + 1024 * (h + 1)].rearrange(
                                "c p t -> p c t")), "LD", 16)
                            c.inc(e.dma_start(out=vblk[h][:], in_=ap["VTM"][
                                kb0 + 1024 * h:kb0 + 1024 * (h + 1), :].rearrange(
                                "(n p) d -> p n d", p=128)), "LD", 16)
                kv_tok = c.cnt["LD"]

                last_sc = 0
                for j in range(8):
                    with nc.Block() as blk:
                        @blk.tensor
                        def _(e):
                            c.w(e, "LD", max(kv_tok, qf_tok))
                            for h in range(2):
                                if h == 0:
                                    c.w(e, "VC", state["sc_evac_v"])
                                else:
                                    c.w(e, "SC", state["sc_evac_s"])
                                for b2 in range(2):
                                    for ch in range(8):
                                        ins = e.matmul(
                                            out=ps_s[h][:, 512 * b2:512 * (b2 + 1)],
                                            lhsT=qf[j][:, ch, :],
                                            rhs=kblk[h][:, ch, 512 * b2:512 * (b2 + 1)],
                                            start=(ch == 0), stop=(ch == 7))
                                c.inc(ins, "TC")
                    sc_tok = c.cnt["TC"]
                    last_sc = sc_tok

                    flush_pending()

                    with nc.Block() as blk:
                        @blk.vector
                        def _(e):
                            c.w(e, "TC", sc_tok - 1)
                            c.w(e, "VC", init_tok)
                            c.w(e, "SC", state["p_tok"])
                            c.inc(e.tensor_copy(out=S_sb[:, :1024], in_=ps_s[0][:]),
                                  "VC")
                        state["sc_evac_v"] = c.cnt["VC"]

                        @blk.scalar
                        def _(e):
                            c.w(e, "TC", sc_tok)
                            c.inc(e.copy(out=S_sb[:, 1024:], in_=ps_s[1][:]), "SC")
                        state["sc_evac_s"] = c.cnt["SC"]

                        old, new = sbi % 2, 1 - sbi % 2
                        @blk.vector
                        def _(e):
                            c.w(e, "SC", state["sc_evac_s"])
                            c.vs(e, e.reduce_max(out=nmb[:], in_=S_sb[:], axis=AX.X,
                                                 negate=True))
                            c.vs(e, e.tensor_tensor(out=nm[:, j, new:new + 1],
                                                    in0=nm[:, j, old:old + 1],
                                                    in1=nmb[:], op=ALU.min))
                            c.inc(e.tensor_tensor(out=dd[:], in0=nm[:, j, new:new + 1],
                                                  in1=nm[:, j, old:old + 1],
                                                  op=ALU.subtract), "VC")
                        sm_tok = c.cnt["VC"]

                        @blk.scalar
                        def _(e):
                            c.w(e, "VC", sm_tok)
                            c.inc(e.activation(out=corr[:], in_=dd[:], func=AF.Exp),
                                  "SC")
                            c.inc(e.activation(out=P_sb[:], in_=S_sb[:], func=AF.Exp,
                                               bias=nm[:, j, new:new + 1],
                                               accum_out=lb[:]), "SC")
                        state["p_tok"] = c.cnt["SC"]

                        @blk.vector
                        def _(e):
                            c.w(e, "SC", state["p_tok"])
                            c.vs(e, e.tensor_scalar(out=ll[:, j:j + 1],
                                                    in0=ll[:, j:j + 1],
                                                    scalar1=corr[:], scalar2=None,
                                                    op0=ALU.mult))
                            c.vs(e, e.tensor_tensor(out=ll[:, j:j + 1],
                                                    in0=ll[:, j:j + 1],
                                                    in1=lb[:], op=ALU.add))
                            c.inc(e.tensor_scalar(out=O[j][:], in0=O[j][:],
                                                  scalar1=corr[:], scalar2=None,
                                                  op0=ALU.mult), "VC")
                    state["pend"] = j
                flush_pending()
                kv_free = (last_sc, state["pv_done"])

            with nc.Block() as blk:
                @blk.vector
                def _(e):
                    c.w(e, "VC", state["pv_done"])
                    c.inc(e.reciprocal(out=linv[:], in_=ll[:]), "VC")
                    c.w(e, "VC")
            oT_free = state["at_evac"]
            last_stile = 0
            st_tok = [0, 0]
            for j in range(8):
                jj = 8 * g + j
                with nc.Block() as blk:
                    @blk.sync
                    def _(e):
                        c.inc(e.dma_start(out=xnt[j % 2][:],
                                          in_=ap["XN"][128 * jj:128 * (jj + 1), :]),
                              "LD", 16)
                    xn_tok = c.cnt["LD"]

                    @blk.vector
                    def _(e):
                        if j >= 1:
                            c.w(e, "TC")
                        c.inc(e.tensor_scalar(out=ob[:], in0=O[j][:],
                                              scalar1=linv[:, j:j + 1], scalar2=None,
                                              op0=ALU.mult), "VC")
                    ob_tok = c.cnt["VC"]

                    @blk.tensor
                    def _(e):
                        c.w(e, "VC", ob_tok)
                        c.w(e, "SC", oT_free)
                        for ch in range(8):
                            ins = e.transpose(out=ps_at[:, ch, :],
                                              in_=ob[:, 128 * ch:128 * (ch + 1)],
                                              identity=cx.identb[:])
                        c.inc(ins, "TC")
                    trj = c.cnt["TC"]

                    @blk.scalar
                    def _(e):
                        c.w(e, "TC", trj)
                        c.inc(e.copy(out=oT[:], in_=ps_at[:, :8, :]), "SC")
                    oT_free = c.cnt["SC"]

                    @blk.tensor
                    def _(e):
                        c.w(e, "SC", oT_free)
                        c.w(e, "VC", last_stile)
                        for h in range(2):
                            for ch in range(8):
                                ins = e.matmul(out=ps_s[0][:, 512 * h:512 * (h + 1)],
                                               lhsT=oT[:, ch, :],
                                               rhs=wp[:, ch, 512 * h:512 * (h + 1)],
                                               start=(ch == 0), stop=(ch == 7))
                        c.inc(ins, "TC")
                    mmj = c.cnt["TC"]

                    @blk.vector
                    def _(e):
                        c.w(e, "TC", mmj)
                        c.w(e, "LD", xn_tok)
                        if g == 0 and j == 0:
                            c.w(e, "LD", wp_ld)
                        c.w(e, "ST", st_tok[j % 2])
                        e.tensor_tensor(out=stile[j % 2][:], in0=ps_s[0][:],
                                        in1=bpr[:], op=ALU.add)
                        c.inc(e.tensor_tensor(out=stile[j % 2][:],
                                              in0=stile[j % 2][:],
                                              in1=xnt[j % 2][:], op=ALU.add), "VC")
                    last_stile = c.cnt["VC"]

                    @blk.sync
                    def _(e):
                        c.w(e, "VC", last_stile)
                        c.inc(e.dma_start(out=ap["SOUT"][128 * jj:128 * (jj + 1), :],
                                          in_=stile[j % 2][:]), "ST", 16)
                    st_tok[j % 2] = c.cnt["ST"]
            state["sc_evac_v"] = last_stile
            kv_free = (c.cnt["TC"], state["pv_done"])

        with nc.Block() as blk:
            @blk.sync
            def _(e):
                c.w(e, "ST")


def phaseB(cx, layer):
    nc, ap, c = cx.nc, cx.ap, cx
    AQ0, AK0 = L_AQ0[layer], L_AK0[layer]
    xsrc = ap["XN"] if layer == 0 else ap["X1"]
    xmid = ap["XMID"] if layer == 0 else ap["XMID2"]
    xout = ap["X1"] if layer == 0 else ap["X2"]

    def xrow(tp):
        if layer == 0:
            return 128 * (((tp - PADT) % (NT_S)) if True else tp)
        return 128 * tp

    with ExitStack() as lst:
        w1 = lst.enter_context(nc.sbuf_tensor(f"pb{layer}_w1", [128, 8, 2 * HID], BF16))
        stats = lst.enter_context(nc.sbuf_tensor(f"pb{layer}_stats", [128, NT_L, 2], F32))
        eps5 = lst.enter_context(nc.sbuf_tensor(f"pb{layer}_eps5", [128, 1], F32))

        with ExitStack() as st:
            def sb(name, shape, dt):
                return st.enter_context(nc.sbuf_tensor(f"pb{layer}a_" + name, shape, dt))

            def psum(name, shape, dt=F32):
                return st.enter_context(nc.psum_tensor(f"pb{layer}a_" + name, shape, dt))

            wqkv = sb("wqkv", [128, 8, 3 * INNER], BF16)
            wo = sb("wo", [128, 4, 1024], BF16)
            masks = sb("masks", [128, 5, 640], BF16)
            qsR = sb("qsR", [128, 64], F32)
            ksR = sb("ksR", [128, 64], F32)
            xt = [sb(f"x{i}", [128, 1024], F32) for i in range(2)]
            xp = [sb(f"xp{i}", [128, 128], F32) for i in range(2)]
            xb = sb("xb", [128, 1024], BF16)
            xqT = sb("xqT", [128, 8, 129], BF16)
            qsb = sb("qsb", [128, 512], F32)
            ksb = sb("ksb", [128, 512], F32)
            sqq = sb("sqq", [128, 512], F32)
            ssq = sb("ssq", [128, 8], F32)
            inq = sb("inq", [128, 8], F32)
            ink = sb("ink", [128, 8], F32)
            q2 = sb("q2", [128, 512], F32)
            k2 = sb("k2", [128, 512], F32)
            t1_ = sb("t1", [128, 512], F32)
            u_ = sb("u", [128, 512], F32)
            qxb = sb("qxb", [128, 512], BF16)
            kxb = sb("kxb", [128, 512], BF16)
            qT = sb("qT", [128, 4, 128], BF16)
            kring = sb("kring", [128, 4, 768], BF16)
            vring = sb("vring", [128, 6, 512], BF16)
            Ss = sb("Ss", [128, 640], F32)
            Pb = sb("Pb", [128, 640], BF16)
            at2 = sb("at2", [128, 5, 128], BF16)
            nm8 = sb("nm8", [128, 8], F32)
            lb8 = sb("lb8", [128, 8], F32)
            li8 = sb("li8", [128, 8], F32)
            ob = sb("ob", [128, 512], BF16)
            oT = sb("oT", [128, 4, 128], BF16)
            xm = [sb(f"xm{i}", [128, 1024], F32) for i in range(2)]
            st4 = sb("st4", [128, 4], F32)
            sh2 = sb("sh2", [128, 2], F32)
            pv2 = sb("pv2", [1, 4], F32)
            pv2g = sb("pv2g", [1, 2], F32)
            mtmp = sb("mtmp", [128, 1], F32)
            vtmp = sb("vtmp", [128, 1], F32)
            ps_qkv = psum("qkv", [128, 3, 512])
            ps_t = psum("t", [128, 8, 128], BF16)
            ps_sc = psum("sc", [128, 1024])
            ps_at = psum("at", [128, 5, 128], BF16)
            ps_pv = psum("pv", [128, 512])

            with nc.Block() as blk:
                @blk.gpsimd
                def _(e):
                    c.w(e, "LG")
                    c.inc(e.dma_start(out=wqkv[:], in_=ap["wqkv_b"][layer].rearrange(
                        "(c p) n -> p c n", p=128)), "LG", 16)
                    c.inc(e.dma_start(out=wo[:], in_=ap["wo_b"][layer].rearrange(
                        "(c p) n -> p c n", p=128)), "LG", 16)
                    c.inc(e.dma_start(out=w1[:], in_=ap["w1_b"][layer].rearrange(
                        "(c p) n -> p c n", p=128)), "LG", 16)

                @blk.sync
                def _(e):
                    c.w(e, "ST")
                    c.inc(e.dma_start(out=masks[:], in_=ap["masksB"].rearrange(
                        "m p n -> p m n")), "LD", 16)
                    c.inc(e.dma_start(out=qsR[:], in_=ap["qscale"][layer][None, :]
                                      .broadcast_to([128, 64])), "LD", 16)
                    c.inc(e.dma_start(out=ksR[:], in_=ap["kscale"][layer][None, :]
                                      .broadcast_to([128, 64])), "LD", 16)
                tab_tok = c.cnt["LD"]

                @blk.vector
                def _(e):
                    e.memset(eps5[:], 1e-5)
                    e.memset(pv2[:], 0.0)
                    e.memset(pv2g[:], 0.0)
                    c.w(e, "LD", tab_tok)
                    c.inc(e.tensor_scalar(out=qsR[:], in0=qsR[:], scalar1=8.0,
                                          scalar2=None, op0=ALU.mult), "VC")
            w_tok = c.cnt["LG"]
            pro_tok = c.cnt["VC"]

            ld_tok = {}
            x_free = {}

            def issue(e, tp):
                if tp >= NT_L:
                    return
                if tp - 2 in x_free:
                    c.w(e, "VC", x_free[tp - 2])
                bb = tp % 2
                c.inc(e.dma_start(out=xt[bb][:],
                                  in_=xsrc[xrow(tp):xrow(tp) + 128, :]), "LD", 16)
                c.inc(e.dma_start(out=xp[bb][:],
                                  in_=ap["xposP"][128 * tp:128 * (tp + 1), :]),
                      "LD", 16)
                ld_tok[tp] = c.cnt["LD"]

            toks = {"xqT_evac": 0, "qkv_evac": 0, "ring_evac": 0, "pst_free": 0,
                    "scm": [0, 0], "p_done": 0, "at_done": 0, "pv_evac": 0,
                    "xm_st": [0, 0], "stats_done": 0, "wo_evac": 0, "l2_done": 0}

            for tp in range(AK0, NT_L):
                b = tp % 2
                qtile = tp >= AQ0
                sl = tp % 6
                with nc.Block() as blk:
                    if tp == AK0:
                        @blk.sync
                        def _(e):
                            issue(e, AK0)
                            issue(e, AK0 + 1)

                    @blk.vector
                    def _(e):
                        c.w(e, "LD", ld_tok[tp])
                        c.w(e, "VC", pro_tok)
                        c.inc(e.tensor_copy(out=xb[:], in_=xt[b][:]), "VC")
                        c.w(e, "SC", toks["xqT_evac"])
                        c.w(e, "VC")
                        if tp == AK0:
                            ins = e.memset(xqT[:, :, 0:1], 0.0)
                        elif tp == GT:
                            ins = e.tensor_scalar(out=xqT[:, :, 0:1],
                                                  in0=xqT[:, :, 128:129],
                                                  scalar1=cx.gate[:], scalar2=None,
                                                  op0=ALU.mult)
                        else:
                            ins = e.tensor_copy(out=xqT[:, :, 0:1],
                                                in_=xqT[:, :, 128:129])
                        c.inc(ins, "VC")
                    stage_tok = c.cnt["VC"]

                    @blk.tensor
                    def _(e):
                        c.w(e, "VC", stage_tok)
                        c.w(e, "SC", toks["ring_evac"])
                        c.w(e, "SC", toks["xqT_evac"])
                        for ch in range(8):
                            ins = e.transpose(out=ps_t[:, ch, :],
                                              in_=xb[:, 128 * ch:128 * (ch + 1)],
                                              identity=cx.identb[:])
                        c.inc(ins, "TC")
                    trx_tok = c.cnt["TC"]

                    @blk.scalar
                    def _(e):
                        c.w(e, "TC", trx_tok)
                        c.w(e, "VC", stage_tok)
                        c.inc(e.copy(out=xqT[:, :, 1:129], in_=ps_t[:]), "SC")
                    toks["xqT_evac"] = c.cnt["SC"]

                    @blk.tensor
                    def _(e):
                        c.w(e, "SC", toks["xqT_evac"])
                        if tp == AK0:
                            c.w(e, "LG", w_tok)
                        c.w(e, "VC", toks["qkv_evac"])
                        for oi, c0 in ((0, 0), (1, 512), (2, 1024)):
                            for ch in range(8):
                                lhs = xqT[:, ch, 1:129] if ch < 4 else xqT[:, ch, 0:128]
                                ins = e.matmul(out=ps_qkv[:, oi, :], lhsT=lhs,
                                               rhs=wqkv[:, ch, c0:c0 + 512],
                                               start=(ch == 0), stop=(ch == 7))
                        c.inc(ins, "TC")
                    mm_tok = c.cnt["TC"]

                    @blk.scalar
                    def _(e):
                        c.w(e, "TC", mm_tok)
                        e.copy(out=qsb[:], in_=ps_qkv[:, 0, :])
                        c.inc(e.copy(out=ksb[:], in_=ps_qkv[:, 1, :]), "SC")
                    qk_tok = c.cnt["SC"]

                    @blk.vector
                    def _(e):
                        c.w(e, "TC", mm_tok)
                        c.w(e, "TC")
                        c.inc(e.tensor_copy(out=vring[:, sl, :], in_=ps_qkv[:, 2, :]),
                              "VC")
                    toks["qkv_evac"] = c.cnt["VC"]

                    def norml2(eng_s, eng_v, src_, invout, scaleR, sqtok):
                        pass

                    @blk.scalar
                    def _(e):
                        c.w(e, "SC", qk_tok)
                        c.w(e, "VC", toks["l2_done"])
                        c.inc(e.activation(out=sqq[:], in_=ksb[:], func=AF.Square),
                              "SC")
                    sqk_tok = c.cnt["SC"]

                    @blk.vector
                    def _(e):
                        c.w(e, "SC", sqk_tok)
                        c.vs(e, e.reduce_sum(out=ssq[:, :, None], in_=sqq[:].rearrange(
                            "p (h d) -> p h d", d=64), axis=AX.X))
                        c.inc(e.tensor_scalar(out=ssq[:], in0=ssq[:], scalar1=1e-24,
                                              scalar2=None, op0=ALU.max), "VC")
                    ssk_tok = c.cnt["VC"]

                    @blk.scalar
                    def _(e):
                        c.w(e, "VC", ssk_tok)
                        e.activation(out=sqq[:, :8], in_=ssq[:], func=AF.Ln)
                        c.inc(e.activation(out=ink[:], in_=sqq[:, :8], func=AF.Exp,
                                           scale=-0.5), "SC")
                    invk_tok = c.cnt["SC"]

                    if qtile:
                        @blk.scalar
                        def _(e):
                            c.inc(e.activation(out=sqq[:], in_=qsb[:], func=AF.Square),
                                  "SC")
                        sqq_tok = c.cnt["SC"]

                        @blk.vector
                        def _(e):
                            c.w(e, "SC", sqq_tok)
                            c.vs(e, e.reduce_sum(out=ssq[:, :, None],
                                                 in_=sqq[:].rearrange(
                                                     "p (h d) -> p h d", d=64),
                                                 axis=AX.X))
                            c.inc(e.tensor_scalar(out=ssq[:], in0=ssq[:],
                                                  scalar1=1e-24, scalar2=None,
                                                  op0=ALU.max), "VC")
                        ssqq_tok = c.cnt["VC"]

                        @blk.scalar
                        def _(e):
                            c.w(e, "VC", ssqq_tok)
                            e.activation(out=sqq[:, :8], in_=ssq[:], func=AF.Ln)
                            c.inc(e.activation(out=inq[:], in_=sqq[:, :8],
                                               func=AF.Exp, scale=-0.5), "SC")
                        invq_tok = c.cnt["SC"]

                    def xpos(e, src_, inv_, scR, coff, soff, sign_swap, dstb):
                        e.tensor_tensor(out=q2[:].rearrange("p (h d) -> p h d", d=64),
                                        in0=src_[:].rearrange("p (h d) -> p h d", d=64),
                                        in1=inv_[:, :, None].broadcast_to([128, 8, 64]),
                                        op=ALU.mult)
                        e.tensor_tensor(out=q2[:].rearrange("p (h d) -> p h d", d=64),
                                        in0=q2[:].rearrange("p (h d) -> p h d", d=64),
                                        in1=scR[:, None, :].broadcast_to([128, 8, 64]),
                                        op=ALU.mult)
                        v4 = lambda t: t[:].rearrange("p (h r d) -> p h r d", r=2, d=32)
                        e.tensor_tensor(out=v4(t1_), in0=v4(q2),
                                        in1=xp[b][:, None, None, coff:coff + 32]
                                        .broadcast_to([128, 8, 2, 32]), op=ALU.mult)
                        e.tensor_tensor(out=v4(u_)[:, :, 0, :], in0=v4(q2)[:, :, 1, :],
                                        in1=xp[b][:, None, soff:soff + 32]
                                        .broadcast_to([128, 8, 32]), op=ALU.mult)
                        e.tensor_tensor(out=v4(u_)[:, :, 1, :], in0=v4(q2)[:, :, 0, :],
                                        in1=xp[b][:, None, soff:soff + 32]
                                        .broadcast_to([128, 8, 32]), op=ALU.mult)
                        e.tensor_tensor(out=v4(dstb)[:, :, 0, :], in0=v4(t1_)[:, :, 0, :],
                                        in1=v4(u_)[:, :, 0, :], op=ALU.subtract)
                        return e.tensor_tensor(out=v4(dstb)[:, :, 1, :],
                                               in0=v4(t1_)[:, :, 1, :],
                                               in1=v4(u_)[:, :, 1, :], op=ALU.add)

                    @blk.vector
                    def _(e):
                        c.w(e, "SC", invk_tok)
                        ins = xpos(e, ksb, ink, ksR, 64, 96, True, kxb)
                        c.inc(ins, "VC")
                        if qtile:
                            c.w(e, "SC", invq_tok)
                            ins = xpos(e, qsb, inq, qsR, 0, 32, True, qxb)
                            c.inc(ins, "VC")
                    toks["l2_done"] = c.cnt["VC"]
                    rope_tok = c.cnt["VC"]

                    @blk.tensor
                    def _(e):
                        c.w(e, "VC", rope_tok)
                        c.w(e, "SC", toks["ring_evac"])
                        for ch in range(4):
                            ins = e.transpose(out=ps_t[:, ch, :],
                                              in_=kxb[:, 128 * ch:128 * (ch + 1)],
                                              identity=cx.identb[:])
                        if qtile:
                            for ch in range(4):
                                ins = e.transpose(out=ps_t[:, 4 + ch, :],
                                                  in_=qxb[:, 128 * ch:128 * (ch + 1)],
                                                  identity=cx.identb[:])
                        c.inc(ins, "TC")
                    trk_tok = c.cnt["TC"]

                    @blk.scalar
                    def _(e):
                        c.w(e, "TC", trk_tok)
                        ins = e.copy(out=kring[:, :, 128 * sl:128 * (sl + 1)],
                                     in_=ps_t[:, 0:4, :])
                        if qtile:
                            ins = e.copy(out=qT[:], in_=ps_t[:, 4:8, :])
                        c.inc(ins, "SC")
                    toks["ring_evac"] = c.cnt["SC"]
                    ring_tok = c.cnt["SC"]

                x_free[tp] = rope_tok
                if not qtile:
                    with nc.Block() as blk:
                        @blk.sync
                        def _(e):
                            issue(e, tp + 2)
                    continue

                slots = [(tp - 4 + i) % 6 for i in range(5)]
                runs = []
                for s in slots:
                    if runs and runs[-1][0] + runs[-1][1] == s:
                        runs[-1] = (runs[-1][0], runs[-1][1] + 1)
                    else:
                        runs.append((s, 1))
                mi = tp - GT if (GT <= tp < GT + 4) else 4

                for hh in range(8):
                    p0 = 64 * (hh % 2)
                    chq = hh // 2
                    with nc.Block() as blk:
                        @blk.tensor
                        def _(e):
                            c.w(e, "SC", ring_tok)
                            c.w(e, "VC", toks["scm"][0])
                            for i5 in range(5):
                                ins = e.matmul(
                                    out=ps_sc[:, 128 * i5:128 * (i5 + 1)],
                                    lhsT=qT[p0:p0 + 64, chq, :],
                                    rhs=kring[p0:p0 + 64, chq,
                                              128 * slots[i5]:128 * (slots[i5] + 1)],
                                    start=True, stop=True)
                            c.inc(ins, "TC")
                        sc_tok = c.cnt["TC"]

                        @blk.vector
                        def _(e):
                            c.w(e, "TC", sc_tok)
                            c.w(e, "SC", toks["p_done"])
                            c.inc(e.scalar_tensor_tensor(
                                out=Ss[:], in0=ps_sc[:, :640], scalar=1.0,
                                in1=masks[:, mi, :], op0=ALU.mult, op1=ALU.add),
                                "VC")
                            c.vs(e, e.reduce_max(out=nm8[:, hh:hh + 1], in_=Ss[:],
                                                 axis=AX.X, negate=True))
                        toks["scm"][0] = c.cnt["VC"] - 1
                        msk_tok = c.cnt["VC"]

                        @blk.scalar
                        def _(e):
                            c.w(e, "VC", msk_tok)
                            c.w(e, "TC", toks["at_done"])
                            c.inc(e.activation(out=Pb[:], in_=Ss[:], func=AF.Exp,
                                               bias=nm8[:, hh:hh + 1],
                                               accum_out=lb8[:, hh:hh + 1]), "SC")
                        toks["p_done"] = c.cnt["SC"]

                        @blk.tensor
                        def _(e):
                            c.w(e, "SC", toks["p_done"])
                            c.w(e, "SC", toks["at_evac"] if "at_evac" in toks else 0)
                            for i5 in range(5):
                                ins = e.transpose(out=ps_at[:, i5, :],
                                                  in_=Pb[:, 128 * i5:128 * (i5 + 1)],
                                                  identity=cx.identb[:])
                            c.inc(ins, "TC")
                        toks["at_done"] = c.cnt["TC"]

                        @blk.scalar
                        def _(e):
                            c.w(e, "TC", toks["at_done"])
                            c.inc(e.copy(out=at2[:], in_=ps_at[:]), "SC")
                        toks["at_evac"] = c.cnt["SC"]

                        @blk.tensor
                        def _(e):
                            c.w(e, "SC", toks["at_evac"])
                            if hh == 0:
                                c.w(e, "VC", toks["pv_evac"])
                            for i5 in range(5):
                                ins = e.matmul(out=ps_pv[:, 64 * hh:64 * (hh + 1)],
                                               lhsT=at2[:, i5, :],
                                               rhs=vring[:, slots[i5],
                                                         64 * hh:64 * (hh + 1)],
                                               start=(i5 == 0), stop=(i5 == 4))
                            c.inc(ins, "TC")
                pv_tok = c.cnt["TC"]

                with nc.Block() as blk:
                    @blk.vector
                    def _(e):
                        c.w(e, "TC", pv_tok)
                        c.vs(e, e.reciprocal(out=li8[:], in_=lb8[:]))
                        c.inc(e.tensor_tensor(
                            out=ob[:].rearrange("p (h d) -> p h d", d=64),
                            in0=ps_pv[:].rearrange("p (h d) -> p h d", d=64),
                            in1=li8[:, :, None].broadcast_to([128, 8, 64]),
                            op=ALU.mult), "VC")
                    toks["pv_evac"] = c.cnt["VC"]
                    ob_tok = c.cnt["VC"]

                    @blk.tensor
                    def _(e):
                        c.w(e, "VC", ob_tok)
                        c.w(e, "SC", toks["ring_evac"])
                        for ch in range(4):
                            ins = e.transpose(out=ps_t[:, ch, :],
                                              in_=ob[:, 128 * ch:128 * (ch + 1)],
                                              identity=cx.identb[:])
                        c.inc(ins, "TC")
                    tro_tok = c.cnt["TC"]

                    @blk.scalar
                    def _(e):
                        c.w(e, "TC", tro_tok)
                        c.inc(e.copy(out=oT[:], in_=ps_t[:, 0:4, :]), "SC")
                    oT_tok = c.cnt["SC"]
                    toks["ring_evac"] = oT_tok

                    @blk.tensor
                    def _(e):
                        c.w(e, "SC", oT_tok)
                        c.w(e, "VC", toks["wo_evac"])
                        for h2 in range(2):
                            for ch in range(4):
                                ins = e.matmul(out=ps_sc[:, 512 * h2:512 * (h2 + 1)],
                                               lhsT=oT[:, ch, :],
                                               rhs=wo[:, ch, 512 * h2:512 * (h2 + 1)],
                                               start=(ch == 0), stop=(ch == 3))
                        c.inc(ins, "TC")
                    womm_tok = c.cnt["TC"]

                    @blk.vector
                    def _(e):
                        c.w(e, "TC", womm_tok)
                        c.w(e, "ST", toks["xm_st"][b])
                        c.inc(e.tensor_tensor(out=xm[b][:], in0=ps_sc[:],
                                              in1=xt[b][:], op=ALU.add), "VC")
                    toks["wo_evac"] = c.cnt["VC"]
                    toks["scm"][0] = max(toks["scm"][0], c.cnt["VC"])
                    xm_tok = c.cnt["VC"]

                    @blk.sync
                    def _(e):
                        c.w(e, "VC", xm_tok)
                        c.inc(e.dma_start(out=xmid[128 * tp:128 * (tp + 1), :],
                                          in_=xm[b][:]), "ST", 16)
                    toks["xm_st"][b] = c.cnt["ST"]
                    x_free[tp] = xm_tok

                    @blk.vector
                    def _(e):
                        c.w(e, "VC", xm_tok)
                        c.w(e, "VC", toks["stats_done"])
                        e.reduce_sum(out=st4[:, 2:3], in_=xm[b][:, :512], axis=AX.X)
                        c.vs(e, e.reduce_sum(out=st4[:, 0:1], in_=xm[b][:, 512:],
                                             axis=AX.X))
                    s12_tok = c.cnt["VC"]

                    @blk.scalar
                    def _(e):
                        c.w(e, "VC", xm_tok)
                        e.activation(out=sqq[:], in_=xm[b][:, :512], func=AF.Square,
                                     accum_out=st4[:, 3:4])
                        c.inc(e.activation(out=sqq[:], in_=xm[b][:, 512:],
                                           func=AF.Square, accum_out=st4[:, 1:2]),
                              "SC")
                    q12_tok = c.cnt["SC"]

                    @blk.tensor
                    def _(e):
                        c.w(e, "VC", s12_tok)
                        c.w(e, "SC", q12_tok)
                        c.inc(e.matmul(out=ps_pv[:, 0:2], lhsT=cx.shiftm[:],
                                       rhs=st4[:, 0:2], start=True, stop=True), "TC")
                    shm_tok = c.cnt["TC"]

                    @blk.gpsimd
                    def _(e):
                        c.w(e, "VC", s12_tok)
                        c.w(e, "SC", q12_tok)
                        c.inc(e.dma_start(out=pv2[0:1, 2 * (tp % 2):2 * (tp % 2) + 2],
                                          in_=st4[127:128, 0:2],
                                          allow_slow_non_contiguous=True), "LG", 16)
                    pv2_tok = c.cnt["LG"]
                    toks["pv2_tok"] = pv2_tok

                    @blk.vector
                    def _(e):
                        c.w(e, "TC", shm_tok)
                        c.w(e, "LG", toks.get("pv2_prev", 0))
                        c.vs(e, e.tensor_copy(out=sh2[:], in_=ps_pv[:, 0:2]))
                        ps_ = 2 * ((tp - 1) % 2)
                        prev_ap = pv2[0:1, ps_:ps_ + 2]
                        if tp == AQ0:
                            prev_ap = pv2g[:]
                        elif tp == GT:
                            c.vs(e, e.tensor_scalar(out=pv2g[:], in0=prev_ap,
                                                    scalar1=cx.gate[0:1, :],
                                                    scalar2=None, op0=ALU.mult))
                            prev_ap = pv2g[:]
                        c.vs(e, e.tensor_tensor(out=sh2[0:1, :], in0=sh2[0:1, :],
                                                in1=prev_ap, op=ALU.add))
                        c.vs(e, e.tensor_tensor(out=mtmp[:], in0=st4[:, 2:3],
                                                in1=sh2[:, 0:1], op=ALU.add))
                        c.vs(e, e.tensor_scalar(out=stats[:, tp, 0:1], in0=mtmp[:],
                                                scalar1=1.0 / 1024, scalar2=None,
                                                op0=ALU.mult))
                        c.vs(e, e.tensor_tensor(out=vtmp[:], in0=st4[:, 3:4],
                                                in1=sh2[:, 1:2], op=ALU.add))
                        c.vs(e, e.tensor_scalar(out=vtmp[:], in0=vtmp[:],
                                                scalar1=1.0 / 1024, scalar2=None,
                                                op0=ALU.mult))
                        c.vs(e, e.tensor_tensor(out=mtmp[:], in0=stats[:, tp, 0:1],
                                                in1=stats[:, tp, 0:1], op=ALU.mult))
                        c.inc(e.tensor_tensor(out=vtmp[:], in0=vtmp[:], in1=mtmp[:],
                                              op=ALU.subtract), "VC")
                    var2_tok = c.cnt["VC"]
                    toks["pv_evac"] = max(toks["pv_evac"], c.cnt["VC"])

                    @blk.scalar
                    def _(e):
                        c.w(e, "VC", var2_tok)
                        e.activation(out=vtmp[:], in_=vtmp[:], func=AF.Ln,
                                     bias=eps5[:])
                        c.inc(e.activation(out=stats[:, tp, 1:2], in_=vtmp[:],
                                           func=AF.Exp, scale=-0.5), "SC")
                    toks["stats_done"] = c.cnt["VC"]
                    toks["pv2_prev"] = toks["pv2_tok"]
                    stats_sc = c.cnt["SC"]
                with nc.Block() as blk:
                    @blk.sync
                    def _(e):
                        issue(e, tp + 2)

            with nc.Block() as blk:
                @blk.sync
                def _(e):
                    c.w(e, "ST")
                    c.w(e, "LG")

        with ExitStack() as st:
            def sb(name, shape, dt):
                return st.enter_context(nc.sbuf_tensor(f"pb{layer}f_" + name, shape, dt))

            def psum(name, shape, dt=F32):
                return st.enter_context(nc.psum_tensor(f"pb{layer}f_" + name, shape, dt))

            w2 = sb("w2", [128, 22, 1024], BF16)
            B1R = sb("B1R", [128, 2 * HID], F32)
            B2R = sb("B2R", [128, 1024], F32)
            lgR = sb("lgR", [128, 1024], F32)
            lbR = sb("lbR", [128, 1024], F32)
            xmt = [sb(f"xm{i}", [128, 1024], F32) for i in range(2)]
            xmb = sb("xmb", [128, 1024], BF16)
            xmT = sb("xmT", [128, 8, 129], BF16)
            x_sh = sb("xsh", [128, 1024], BF16)
            xhb = xmb
            xhT = sb("xhT", [128, 8, 128], BF16)
            as_ = sb("as", [128, 455], F32)
            gs = sb("gs", [128, 455], F32)
            xh_buf = sb("xh", [128, 1024], F32)
            gel = gs
            h_bf = sb("hbf", [128, HID], BF16)
            hT = sb("hT", [128, 22, 128], BF16)
            x2t = [sb(f"x2{i}", [128, 1024], F32) for i in range(2)]
            ps_t = psum("t", [128, 8, 128], BF16)
            ps_w1 = psum("psw1", [128, 2, 512])
            ps_hT = psum("pshT", [128, 22, 128], BF16)
            ps_w2 = psum("psw2", [128, 1024])

            with nc.Block() as blk:
                @blk.gpsimd
                def _(e):
                    c.w(e, "LG")
                    c.inc(e.dma_start(out=w2[:, 0:21, :], in_=ap["w2_b"][layer][0:2688]
                                      .rearrange("(c p) n -> p c n", p=128)), "LG", 16)
                    c.inc(e.dma_start(out=w2[0:42, 21, :],
                                      in_=ap["w2_b"][layer][2688:2730]), "LG", 16)

                @blk.sync
                def _(e):
                    c.w(e, "ST")
                    c.inc(e.dma_start(out=B1R[:], in_=ap["b1_b"][layer][None, :]
                                      .broadcast_to([128, 2 * HID])), "LD", 16)
                    c.inc(e.dma_start(out=B2R[:], in_=ap["b2_b"][layer][None, :]
                                      .broadcast_to([128, 1024])), "LD", 16)
                    c.inc(e.dma_start(out=lgR[:], in_=ap["lng"][layer][None, :]
                                      .broadcast_to([128, 1024])), "LD", 16)
                    c.inc(e.dma_start(out=lbR[:], in_=ap["lnb"][layer][None, :]
                                      .broadcast_to([128, 1024])), "LD", 16)
            wf_lg = c.cnt["LG"]
            wf_ld = c.cnt["LD"]

            ld_tok = {}
            xm_free = {}

            def issue2(e, tp):
                if tp >= NT_L:
                    return
                if tp - 2 in xm_free:
                    c.w(e, "VC", xm_free[tp - 2])
                c.inc(e.dma_start(out=xmt[tp % 2][:],
                                  in_=xmid[128 * tp:128 * (tp + 1), :]), "LD", 16)
                ld_tok[tp] = c.cnt["LD"]

            t2 = {"xmT_evac": 0, "xsh_evac": 0, "xhT_evac": 0, "w1_evac": 0,
                  "gel_done": 0, "h_cast": 0, "hT_evac": 0, "w2_evac": 0,
                  "x2_st": [0, 0]}

            for tp in range(AQ0, NT_L):
                b = tp % 2
                with nc.Block() as blk:
                    if tp == AQ0:
                        @blk.sync
                        def _(e):
                            issue2(e, AQ0)
                            issue2(e, AQ0 + 1)

                    @blk.vector
                    def _(e):
                        c.w(e, "LD", ld_tok[tp])
                        c.w(e, "TC", t2.get("tr3_prev", 0))
                        c.inc(e.tensor_copy(out=xmb[:], in_=xmt[b][:]), "VC")
                        c.w(e, "SC", t2["xmT_evac"])
                        c.w(e, "VC")
                        if tp == AQ0:
                            ins = e.memset(xmT[:, :, 0:1], 0.0)
                        elif tp == GT:
                            ins = e.tensor_scalar(out=xmT[:, :, 0:1],
                                                  in0=xmT[:, :, 128:129],
                                                  scalar1=cx.gate[:], scalar2=None,
                                                  op0=ALU.mult)
                        else:
                            ins = e.tensor_copy(out=xmT[:, :, 0:1],
                                                in_=xmT[:, :, 128:129])
                        c.inc(ins, "VC")
                    stage_tok = c.cnt["VC"]

                    @blk.tensor
                    def _(e):
                        c.w(e, "VC", stage_tok)
                        c.w(e, "SC", t2["xmT_evac"])
                        for ch in range(8):
                            ins = e.transpose(out=ps_t[:, ch, :],
                                              in_=xmb[:, 128 * ch:128 * (ch + 1)],
                                              identity=cx.identb[:])
                        c.inc(ins, "TC")
                    tr1_tok = c.cnt["TC"]

                    @blk.scalar
                    def _(e):
                        c.w(e, "TC", tr1_tok)
                        c.w(e, "VC", stage_tok)
                        c.inc(e.copy(out=xmT[:, :, 1:129], in_=ps_t[:]), "SC")
                    t2["xmT_evac"] = c.cnt["SC"]

                    @blk.tensor
                    def _(e):
                        c.w(e, "SC", t2["xmT_evac"])
                        c.w(e, "SC", t2["xsh_evac"])
                        for ch in range(8):
                            src_ = xmT[:, ch, 1:129] if ch < 4 else xmT[:, ch, 0:128]
                            ins = e.transpose(out=ps_t[:, ch, :], in_=src_,
                                              identity=cx.identb[:])
                        c.inc(ins, "TC")
                    tr2_tok = c.cnt["TC"]

                    @blk.scalar
                    def _(e):
                        c.w(e, "TC", tr2_tok)
                        c.inc(e.copy(out=x_sh[:], in_=ps_t[:]), "SC")
                    t2["xsh_evac"] = c.cnt["SC"]
                    xsh_tok = c.cnt["SC"]

                    @blk.vector
                    def _(e):
                        c.w(e, "SC", xsh_tok)
                        if tp == AQ0:
                            c.w(e, "LD", wf_ld)
                        c.w(e, "TC", t2.get("tr3_prev", 0))
                        xh = xh_buf[:]
                        e.tensor_scalar(out=xh, in0=x_sh[:],
                                        scalar1=stats[:, tp, 0:1],
                                        scalar2=stats[:, tp, 1:2],
                                        op0=ALU.subtract, op1=ALU.mult)
                        e.tensor_tensor(out=xh, in0=xh, in1=lgR[:], op=ALU.mult)
                        c.inc(e.tensor_tensor(out=xhb[:], in0=xh, in1=lbR[:],
                                              op=ALU.add), "VC")
                    xh_tok = c.cnt["VC"]

                    @blk.tensor
                    def _(e):
                        c.w(e, "VC", xh_tok)
                        c.w(e, "SC", t2["xhT_evac"])
                        for ch in range(8):
                            ins = e.transpose(out=ps_t[:, ch, :],
                                              in_=xhb[:, 128 * ch:128 * (ch + 1)],
                                              identity=cx.identb[:])
                        c.inc(ins, "TC")
                    t2["tr3_prev"] = c.cnt["TC"]
                    tr3_tok = c.cnt["TC"]

                    @blk.scalar
                    def _(e):
                        c.w(e, "TC", tr3_tok)
                        c.inc(e.copy(out=xhT[:], in_=ps_t[:]), "SC")
                    t2["xhT_evac"] = c.cnt["SC"]
                    xhT_tok = c.cnt["SC"]

                    for p in range(6):
                        @blk.tensor
                        def _(e):
                            c.w(e, "SC", xhT_tok)
                            if tp == AQ0 and p == 0:
                                c.w(e, "LG", wf_lg)
                            c.w(e, "VC", t2["w1_evac"])
                            for ch in range(8):
                                ins = e.matmul(out=ps_w1[:, 0, :455],
                                               lhsT=xhT[:, ch, :],
                                               rhs=w1[:, ch, 455 * p:455 * (p + 1)],
                                               start=(ch == 0), stop=(ch == 7))
                            for ch in range(8):
                                ins = e.matmul(
                                    out=ps_w1[:, 1, :455], lhsT=xhT[:, ch, :],
                                    rhs=w1[:, ch, HID + 455 * p:HID + 455 * (p + 1)],
                                    start=(ch == 0), stop=(ch == 7))
                            c.inc(ins, "TC")
                        w1mm = c.cnt["TC"]

                        @blk.vector
                        def _(e):
                            c.w(e, "TC", w1mm)
                            c.w(e, "SC", t2["gel_done"])
                            e.tensor_tensor(out=gs[:], in0=ps_w1[:, 1, :455],
                                            in1=B1R[:, HID + 455 * p:HID + 455 * (p + 1)],
                                            op=ALU.add)
                            c.inc(e.tensor_tensor(out=as_[:], in0=ps_w1[:, 0, :455],
                                                  in1=B1R[:, 455 * p:455 * (p + 1)],
                                                  op=ALU.add), "VC")
                        ag_tok = c.cnt["VC"]
                        t2["w1_evac"] = ag_tok

                        @blk.scalar
                        def _(e):
                            c.w(e, "VC", ag_tok)
                            c.inc(e.activation(out=gs[:], in_=gs[:], func=AF.Gelu),
                                  "SC")
                        t2["gel_done"] = c.cnt["SC"]

                        @blk.vector
                        def _(e):
                            c.w(e, "SC", t2["gel_done"])
                            if p == 0:
                                c.w(e, "TC", t2.get("tr4_prev", 0))
                            c.inc(e.tensor_tensor(out=h_bf[:, 455 * p:455 * (p + 1)],
                                                  in0=as_[:], in1=gel[:],
                                                  op=ALU.mult), "VC")
                    hb_tok = c.cnt["VC"]

                    @blk.tensor
                    def _(e):
                        c.w(e, "VC", hb_tok)
                        c.w(e, "SC", t2["hT_evac"])
                        for ch in range(22):
                            wdt = 128 if ch < 21 else HID - 128 * 21
                            ins = e.transpose(out=ps_hT[:wdt, ch, :],
                                              in_=h_bf[:, 128 * ch:128 * ch + wdt],
                                              identity=cx.identb[:])
                        c.inc(ins, "TC")
                    t2["tr4_prev"] = c.cnt["TC"]
                    tr4_tok = c.cnt["TC"]

                    @blk.scalar
                    def _(e):
                        c.w(e, "TC", tr4_tok)
                        c.inc(e.copy(out=hT[:], in_=ps_hT[:]), "SC")
                    t2["hT_evac"] = c.cnt["SC"]
                    hT_tok = c.cnt["SC"]

                    @blk.tensor
                    def _(e):
                        c.w(e, "SC", hT_tok)
                        c.w(e, "VC", t2["w2_evac"])
                        for h2 in range(2):
                            for ch in range(22):
                                wdt = 128 if ch < 21 else HID - 128 * 21
                                ins = e.matmul(out=ps_w2[:, 512 * h2:512 * (h2 + 1)],
                                               lhsT=hT[:wdt, ch, :],
                                               rhs=w2[:wdt, ch, 512 * h2:512 * (h2 + 1)],
                                               start=(ch == 0), stop=(ch == 21))
                        c.inc(ins, "TC")
                    w2mm = c.cnt["TC"]

                    @blk.vector
                    def _(e):
                        c.w(e, "TC", w2mm)
                        c.w(e, "ST", t2["x2_st"][b])
                        e.tensor_tensor(out=x2t[b][:], in0=ps_w2[:], in1=B2R[:],
                                        op=ALU.add)
                        c.inc(e.tensor_tensor(out=x2t[b][:], in0=x2t[b][:],
                                              in1=xmt[b][:], op=ALU.add), "VC")
                    t2["w2_evac"] = c.cnt["VC"]
                    x2_tok = c.cnt["VC"]
                    xm_free[tp] = x2_tok

                    @blk.sync
                    def _(e):
                        c.w(e, "VC", x2_tok)
                        c.inc(e.dma_start(out=xout[128 * tp:128 * (tp + 1), :],
                                          in_=x2t[b][:]), "ST", 16)
                        issue2(e, tp + 2)
                    t2["x2_st"][b] = c.cnt["ST"]

            with nc.Block() as blk:
                @blk.sync
                def _(e):
                    c.w(e, "ST")
                    c.w(e, "LG")


def phaseF(cx):
    nc, ap, c = cx.nc, cx.ap, cx
    with ExitStack() as st:
        def sb(name, shape, dt):
            return st.enter_context(nc.sbuf_tensor("pf_" + name, shape, dt))

        x0t = [sb(f"x0{i}", [128, 1024], F32) for i in range(2)]
        s_t = [sb(f"s{i}", [128, 1024], F32) for i in range(2)]
        x2t = [sb(f"x2{i}", [128, 1024], F32) for i in range(2)]
        g_t = sb("g", [128, 1024], F32)
        o_t = [sb(f"o{i}", [128, 1024], F32) for i in range(2)]

        ld_tok = {}
        done = {}
        st_tok = [0, 0]

        def issue(e, i):
            if i >= NT_H:
                return
            if i - 2 in done:
                c.w(e, "VC", done[i - 2])
            bb = i % 2
            c.inc(e.dma_start(out=x0t[bb][:],
                              in_=ap["x0r"][128 * i:128 * (i + 1), :]), "LD", 16)
            c.inc(e.dma_start(out=s_t[bb][:],
                              in_=ap["SOUT"][128 * i:128 * (i + 1), :]), "LD", 16)
            c.inc(e.dma_start(out=x2t[bb][:],
                              in_=ap["X2"][128 * (PADT + i):128 * (PADT + i + 1), :]),
                  "LD", 16)
            ld_tok[i] = c.cnt["LD"]

        for i in range(NT_H):
            b = i % 2
            with nc.Block() as blk:
                if i == 0:
                    @blk.sync
                    def _(e):
                        c.w(e, "ST")
                        issue(e, 0)
                        issue(e, 1)

                @blk.scalar
                def _(e):
                    c.w(e, "LD", ld_tok[i])
                    c.w(e, "VC", done.get(i - 1, 0))
                    c.inc(e.activation(out=g_t[:], in_=x2t[b][:], func=AF.Silu),
                          "SC")
                g_tok = c.cnt["SC"]

                @blk.vector
                def _(e):
                    c.w(e, "SC", g_tok)
                    c.w(e, "ST", st_tok[b])
                    e.tensor_tensor(out=o_t[b][:], in0=s_t[b][:], in1=g_t[:],
                                    op=ALU.mult)
                    c.inc(e.tensor_tensor(out=o_t[b][:], in0=o_t[b][:],
                                          in1=x0t[b][:], op=ALU.add), "VC")
                done[i] = c.cnt["VC"]

                @blk.sync
                def _(e):
                    c.w(e, "VC", done[i])
                    c.inc(e.dma_start(out=ap["out"][128 * i:128 * (i + 1), :],
                                      in_=o_t[b][:]), "ST", 16)
                    issue(e, i + 2)
                st_tok[b] = c.cnt["ST"]


def kernel(**inputs):
    tabs = _tables()
    nc = build_nc(dbg=_DBG)
    x0 = np.asarray(inputs["L0_attn_x"])
    shared = {
        "wqkv_a": np.asarray(inputs["L0_attn_w_qkv"]),
        "bqkv_a": np.asarray(inputs["L0_attn_b_qkv"]),
        "gnorm": np.asarray(inputs["L0_attn_g_norm"]),
        "wproj": np.asarray(inputs["L0_attn_w_proj"]),
        "bproj": np.asarray(inputs["L0_attn_b_proj"]),
        "wqkv_b": np.asarray(inputs["L0_ffn_wqkv"]),
        "qscale": np.asarray(inputs["L0_ffn_q_scale"]),
        "kscale": np.asarray(inputs["L0_ffn_k_scale"]),
        "wo_b": np.asarray(inputs["L0_ffn_wo"]),
        "lng": np.asarray(inputs["L0_ffn_ln_g"]),
        "lnb": np.asarray(inputs["L0_ffn_ln_b"]),
        "w1_b": np.asarray(inputs["L0_ffn_w1"]),
        "b1_b": np.asarray(inputs["L0_ffn_b1"]),
        "w2_b": np.asarray(inputs["L0_ffn_w2"]),
        "b2_b": np.asarray(inputs["L0_ffn_b2"]),
    }
    in_maps = []
    for core in range(8):
        b, h = core // 2, core % 2
        m = dict(shared)
        m["x0r"] = np.ascontiguousarray(np.roll(x0[b], -HALF * h, axis=0))
        m.update(_tables()[h])
        in_maps.append(m)
    res = run_bass_kernel_spmd(nc, in_maps, core_ids=list(range(8)))
    out = np.empty((BB, S, D), dtype=np.float32)
    for core in range(8):
        b, h = core // 2, core % 2
        if "out" in res.results[core]:
            out[b, HALF * h:HALF * (h + 1)] = np.asarray(res.results[core]["out"])
    kernel._last_results = res.results
    return out


# version 02
# speedup vs baseline: 1.0757x; 1.0757x over previous
import os
from contextlib import ExitStack

import numpy as np
import ml_dtypes

import concourse.bass as bass
import concourse.mybir as mybir
from concourse.bass_utils import run_bass_kernel_spmd

F32 = mybir.dt.float32
BF16 = mybir.dt.bfloat16
AF = mybir.ActivationFunctionType
ALU = mybir.AluOpType
AX = mybir.AxisListType

BB, S, D = 4, 8192, 1024
HALF = 4096
NT_S = S // 128
NT_H = HALF // 128
PADT = 10
NT_L = PADT + NT_H
L_AQ0 = (4, 8)
L_AK0 = (0, 4)
GT = PADT
WIN = 512
HEADS, DH = 8, 64
INNER = 512
HID = 2730
LN3 = 1.0986122886681098
SCL_A = LN3 / 32.0
NEG = -3.0e38

_DBG = os.environ.get("MK_DBG", "")
_SERIAL = bool(os.environ.get("MK_SERIAL", ""))
_PHASES = os.environ.get("MK_PHASES", "0,A,B0,B1,F").split(",")


def _host_tables(h):
    QS = HALF * h
    t = np.arange(S, dtype=np.float32)
    inv = (1.0 / 10000.0 ** (np.arange(512, dtype=np.float32) / 512.0)).astype(np.float32)
    ang = t[:, None] * inv[None, :]
    cosr = np.roll(np.round(np.cos(ang)), -QS, axis=0).astype(ml_dtypes.bfloat16)
    sinr = np.roll(np.round(np.sin(ang)), -QS, axis=0).astype(ml_dtypes.bfloat16)
    half_idx = np.arange(0, DH, 2, dtype=np.float32)
    invf = (1.0 / 10000.0 ** (half_idx / DH)).astype(np.float32)
    freqs = t[:, None] * invf[None, :]
    base = ((half_idx + 0.4 * DH) / (1.4 * DH)).astype(np.float32)
    power = ((t - S // 2) / float(WIN // 2)).astype(np.float32)
    scale = (base[None, :] ** power[:, None]).astype(np.float32)
    cosx = np.cos(freqs).astype(np.float32)
    sinx = np.sin(freqs).astype(np.float32)
    gp = (QS + np.arange(NT_L * 128) - PADT * 128) % S
    xposP = np.concatenate(
        [cosx[gp] * scale[gp], sinx[gp] * scale[gp],
         cosx[gp] / scale[gp], sinx[gp] / scale[gp]], axis=1
    ).astype(np.float32)
    r = np.arange(128)[:, None]
    c = np.arange(640)[None, :]
    band = (c >= r) & (c <= r + WIN)
    masks = np.full((5, 128, 640), NEG, dtype=np.float32)
    for k in range(4):
        g0 = 128 * k
        ok = (band & (c >= WIN - g0)) if h == 0 else band
        masks[k][ok] = 0.0
    masks[4][band] = 0.0
    return dict(
        cosr=cosr, sinr=sinr, xposP=xposP,
        masksB=np.ascontiguousarray(masks.astype(ml_dtypes.bfloat16)),
        gate=np.full((128, 1), 0.0 if h == 0 else 1.0, dtype=np.float32),
        shiftm=np.eye(128, k=1, dtype=np.float32),
        identb=np.eye(128).astype(ml_dtypes.bfloat16),
    )


_TABLES = None


def _tables():
    global _TABLES
    if _TABLES is None:
        _TABLES = [_host_tables(0), _host_tables(1)]
    return _TABLES


class Ctx:

    def __init__(self, nc, stack):
        self.nc = nc
        self.stack = stack
        self._sems = {}
        self.cnt = {}
        for name in ("LD", "LG", "ST", "TC", "VC", "SC", "GC"):
            self._sems[name] = stack.enter_context(nc.semaphore(name))
            self.cnt[name] = 0

    def inc(self, ins, name, k=1):
        ins.then_inc(self._sems[name], k)
        self.cnt[name] += k
        return self.cnt[name]

    def w(self, eng, name, v=None):
        if v is None:
            v = self.cnt[name]
        if v > 0:
            eng.wait_ge(self._sems[name], v)

    def vs(self, eng, ins, name="VC"):
        self.inc(ins, name)
        self.w(eng, name)


def build_nc(dbg=""):
    nc = bass.Bass("TRN2", target_bir_lowering=False)
    ap = {}

    def din(name, shape, dt=F32):
        ap[name] = nc.dram_tensor(name, shape, dt, kind="ExternalInput").ap()

    def dout(name, shape, dt=F32):
        ap[name] = nc.dram_tensor(name, shape, dt, kind="ExternalOutput").ap()

    def dscr(name, shape, dt):
        kind = "ExternalOutput" if name in dbg else "Internal"
        ap[name] = nc.dram_tensor(name, shape, dt, kind=kind).ap()

    din("x0r", [S, D])
    din("wqkv_a", [D, 3 * D]); din("bqkv_a", [3 * D]); din("gnorm", [D])
    din("wproj", [D, D]); din("bproj", [D])
    din("cosr", [S, 512], BF16); din("sinr", [S, 512], BF16)
    din("wqkv_b", [2, D, 3 * INNER]); din("qscale", [2, DH]); din("kscale", [2, DH])
    din("wo_b", [2, INNER, D]); din("lng", [2, D]); din("lnb", [2, D])
    din("w1_b", [2, D, 2 * HID]); din("b1_b", [2, 2 * HID])
    din("w2_b", [2, HID, D]); din("b2_b", [2, D])
    din("xposP", [NT_L * 128, 128]); din("masksB", [5, 128, 640], BF16)
    din("gate", [128, 1]); din("shiftm", [128, 128]); din("identb", [128, 128], BF16)
    dout("out", [HALF, D])

    dscr("XN", [S, D], F32)
    dscr("QFM", [8, 128, HALF], BF16)
    dscr("KFM", [8, 128, S], BF16)
    dscr("VTM", [S, D], BF16)
    dscr("SOUT", [HALF, D], F32)
    dscr("XMID", [NT_L * 128, D], F32)
    dscr("X1", [NT_L * 128, D], F32)
    dscr("XMID2", [NT_L * 128, D], F32)
    dscr("X2", [NT_L * 128, D], F32)

    with ExitStack() as stack:
        cx = Ctx(nc, stack)
        cx.ap = ap
        cx.identb = stack.enter_context(nc.sbuf_tensor("g_identb", [128, 128], BF16))
        cx.shiftm = stack.enter_context(nc.sbuf_tensor("g_shiftm", [128, 128], F32))
        cx.gate = stack.enter_context(nc.sbuf_tensor("g_gate", [128, 1], F32))
        with nc.Block() as blk:
            @blk.sync
            def _(e):
                cx.inc(e.dma_start(out=cx.identb[:], in_=ap["identb"]), "LD", 16)
                cx.inc(e.dma_start(out=cx.shiftm[:], in_=ap["shiftm"]), "LD", 16)
                cx.inc(e.dma_start(out=cx.gate[:], in_=ap["gate"]), "LD", 16)

        if "0" in _PHASES:
            phase0(cx)
        if "A" in _PHASES:
            phaseA(cx)
        if "B0" in _PHASES:
            phaseB(cx, 0)
        if "B1" in _PHASES:
            phaseB(cx, 1)
        if "F" in _PHASES:
            phaseF(cx)

        with nc.Block() as blk:
            @blk.sync
            def _(e):
                cx.w(e, "ST")
                cx.w(e, "LG")
    return nc


def phase0(cx):
    nc, ap, c = cx.nc, cx.ap, cx
    with ExitStack() as st:
        def sb(name, shape, dt):
            return st.enter_context(nc.sbuf_tensor(name, shape, dt))

        def psum(name, shape, dt=F32):
            return st.enter_context(nc.psum_tensor(name, shape, dt))

        wq = sb("p0_wq", [128, 8, 3072], BF16)
        wtmp = sb("p0_wtmp", [128, 3072], F32)
        gfm = sb("p0_gfm", [128, 8], F32)
        bqr = sb("p0_bqr", [128, 3072], F32)
        xin = [sb(f"p0_x{i}", [128, 1024], F32) for i in range(2)]
        ctab = [sb(f"p0_ct{i}", [128, 512], BF16) for i in range(2)]
        stab = [sb(f"p0_st{i}", [128, 512], BF16) for i in range(2)]
        xn = sb("p0_xn", [128, 1024], F32)
        ha = sb("p0_ha", [128, 1024], BF16)
        hT = sb("p0_hT", [128, 8, 128], BF16)
        qb = sb("p0_qb", [128, 1024], F32)
        kb = sb("p0_kb", [128, 1024], F32)
        vb = sb("p0_vb", [128, 1024], BF16)
        t1 = sb("p0_t1", [128, 1024], F32)
        t2 = sb("p0_t2", [128, 1024], F32)
        qrb = sb("p0_qrb", [128, 1024], BF16)
        krb = sb("p0_krb", [128, 1024], BF16)
        qT = sb("p0_qT", [128, 8, 128], BF16)
        kT = sb("p0_kT", [128, 8, 128], BF16)
        sA = sb("p0_sA", [128, 1], F32)
        sQ = sb("p0_sQ", [128, 1], F32)
        sqd = sb("p0_sqd", [128, 1024], F32)
        m_ = sb("p0_m", [128, 1], F32)
        va = sb("p0_va", [128, 1], F32)
        tt_ = sb("p0_tt", [128, 1], F32)
        va2 = sb("p0_va2", [128, 2], F32)
        lnv2 = sb("p0_lnv2", [128, 2], F32)
        inv2 = sb("p0_inv2", [128, 2], F32)
        ps_q = psum("p0_psq", [128, 1024])
        ps_k = psum("p0_psk", [128, 1024])
        ps_v = psum("p0_psv", [128, 1024])
        ps_t1 = psum("p0_pst1", [128, 8, 128], BF16)
        ps_t2 = psum("p0_pst2", [128, 8, 128], BF16)

        vcW = c.cnt["VC"]
        lgW = c.cnt["LG"]
        with nc.Block() as blk:
            @blk.sync
            def _(e):
                c.inc(e.dma_start(out=gfm[:], in_=ap["gnorm"].rearrange("(c p) -> p c", p=128), allow_slow_non_contiguous=True), "LD", 16)
                c.inc(e.dma_start(out=bqr[:], in_=ap["bqkv_a"][None, :].broadcast_to([128, 3072])), "LD", 16)
            ld0 = c.cnt["LD"]

            @blk.gpsimd
            def _(e):
                for ch in range(8):
                    if ch >= 1:
                        e.wait_ge(c._sems["VC"], vcW + ch)
                    e.dma_start(out=wtmp[:], in_=ap["wqkv_a"][128 * ch:128 * (ch + 1), :]).then_inc(
                        c._sems["LG"], 16)
            c.cnt["LG"] += 16 * 8

            @blk.vector
            def _(e):
                c.w(e, "LD", ld0)
                e.tensor_scalar(out=bqr[:, :1024], in0=bqr[:, :1024], scalar1=float(SCL_A),
                                scalar2=None, op0=ALU.mult)
                for ch in range(8):
                    e.wait_ge(c._sems["LG"], lgW + 16 * (ch + 1))
                    e.tensor_scalar(out=wq[:, ch, :], in0=wtmp[:], scalar1=gfm[:, ch:ch + 1],
                                    scalar2=None, op0=ALU.mult)
                    ins = e.tensor_scalar(out=wq[:, ch, :1024], in0=wq[:, ch, :1024],
                                          scalar1=float(SCL_A), scalar2=None, op0=ALU.mult)
                    ins.then_inc(c._sems["VC"], 1)
            c.cnt["VC"] += 8
        w_ready = c.cnt["VC"]

        ld_tok = [None] * (NT_S + 2)
        free_tok = [0] * NT_S

        def issue(e, i):
            if i >= NT_S:
                return
            bb = i % 2
            if i >= 2:
                c.w(e, "VC", free_tok[i - 2])
            c.inc(e.dma_start(out=xin[bb][:], in_=ap["x0r"][128 * i:128 * (i + 1), :]), "LD", 16)
            c.inc(e.dma_start(out=ctab[bb][:], in_=ap["cosr"][128 * i:128 * (i + 1), :]), "LD", 16)
            c.inc(e.dma_start(out=stab[bb][:], in_=ap["sinr"][128 * i:128 * (i + 1), :]), "LD", 16)
            ld_tok[i] = c.cnt["LD"]

        evac_tok = 0
        hT_free = None
        t2_free = ("SC", 0)
        var_prev = 0
        xn_st_prev = 0
        tr1_prev = 0
        for i in range(NT_S):
            b = i % 2
            own = i < NT_H
            with nc.Block() as blk:
                if i == 0:
                    @blk.sync
                    def _(e):
                        issue(e, 0)
                        issue(e, 1)

                sc0 = c.cnt["SC"]
                sq_tok = sc0 + 1
                inv_tok = sc0 + 2
                vc0 = c.cnt["VC"]
                va2_tok = vc0 + 7

                @blk.scalar
                def _(e):
                    c.w(e, "LD", ld_tok[i])
                    c.w(e, "VC", var_prev)
                    c.inc(e.activation(out=sqd[:], in_=xin[b][:], func=AF.Square,
                                       accum_out=sQ[:]), "SC")
                    c.w(e, "VC", va2_tok)
                    e.activation(out=lnv2[:], in_=va2[:], func=AF.Ln)
                    c.inc(e.activation(out=inv2[:], in_=lnv2[:], func=AF.Exp,
                                       scale=-0.5), "SC")
                assert c.cnt["SC"] == inv_tok

                @blk.vector
                def _(e):
                    c.w(e, "LD", ld_tok[i])
                    c.vs(e, e.reduce_sum(out=sA[:], in_=xin[b][:], axis=AX.X))
                    c.vs(e, e.tensor_scalar(out=m_[:], in0=sA[:], scalar1=1.0 / 1024,
                                            scalar2=None, op0=ALU.mult))
                    c.w(e, "SC", sq_tok)
                    c.vs(e, e.tensor_scalar(out=va[:], in0=sQ[:], scalar1=1.0 / 1024,
                                            scalar2=None, op0=ALU.mult))
                    c.vs(e, e.tensor_tensor(out=tt_[:], in0=m_[:], in1=m_[:],
                                            op=ALU.mult))
                    c.vs(e, e.tensor_tensor(out=va[:], in0=va[:], in1=tt_[:],
                                            op=ALU.subtract))
                    c.vs(e, e.tensor_scalar(out=va2[:, 0:1], in0=va[:], scalar1=1e-5,
                                            scalar2=None, op0=ALU.add))
                    c.inc(e.tensor_scalar(out=va2[:, 1:2], in0=va[:],
                                          scalar1=1.000001, scalar2=1e-11,
                                          op0=ALU.mult, op1=ALU.add), "VC")
                assert c.cnt["VC"] == va2_tok
                var_prev = va2_tok

                @blk.vector
                def _(e):
                    c.w(e, "SC", inv_tok)
                    c.w(e, "ST", xn_st_prev)
                    c.w(e, "TC", tr1_prev)
                    e.tensor_scalar(out=xn[:], in0=xin[b][:], scalar1=m_[:],
                                    scalar2=inv2[:, 0:1],
                                    op0=ALU.subtract, op1=ALU.mult)
                    c.inc(e.tensor_scalar(out=ha[:], in0=xin[b][:], scalar1=m_[:],
                                          scalar2=inv2[:, 1:2],
                                          op0=ALU.subtract, op1=ALU.mult), "VC")
                xn_tok = c.cnt["VC"]

                @blk.sync
                def _(e):
                    c.w(e, "VC", xn_tok)
                    c.inc(e.dma_start(out=ap["XN"][128 * i:128 * (i + 1), :], in_=xn[:]), "ST", 16)
                xn_st_prev = c.cnt["ST"]

                @blk.tensor
                def _(e):
                    c.w(e, "VC", xn_tok)
                    if hT_free:
                        c.w(e, hT_free[0], hT_free[1])
                    for ch in range(8):
                        ins = e.transpose(out=ps_t1[:, ch, :], in_=ha[:, 128 * ch:128 * (ch + 1)],
                                          identity=cx.identb[:])
                    c.inc(ins, "TC")
                tr1_prev = c.cnt["TC"]

                @blk.vector
                def _(e):
                    c.w(e, "TC")
                    c.inc(e.tensor_copy(out=hT[:], in_=ps_t1[:]), "VC")
                hT_free = ("VC", c.cnt["VC"])

                @blk.tensor
                def _(e):
                    c.w(e, hT_free[0], hT_free[1])
                    if i == 0:
                        c.w(e, "VC", w_ready)
                    c.w(e, "VC", evac_tok)
                    cols = ([(ps_q, 0)] if own else []) + [(ps_k, 1024), (ps_v, 2048)]
                    for (pst, c0) in cols:
                        for hf in range(2):
                            for ch in range(8):
                                ins = e.matmul(
                                    out=pst[:, 512 * hf:512 * (hf + 1)],
                                    lhsT=hT[:, ch, :],
                                    rhs=wq[:, ch, c0 + 512 * hf:c0 + 512 * (hf + 1)],
                                    start=(ch == 0), stop=(ch == 7))
                    c.inc(ins, "TC")
                mm_tok = c.cnt["TC"]

                @blk.vector
                def _(e):
                    c.w(e, "TC", mm_tok)
                    if own:
                        e.tensor_tensor(out=qb[:], in0=ps_q[:], in1=bqr[:, :1024], op=ALU.add)
                    e.tensor_tensor(out=kb[:], in0=ps_k[:], in1=bqr[:, 1024:2048], op=ALU.add)
                    c.inc(e.tensor_tensor(out=vb[:], in0=ps_v[:], in1=bqr[:, 2048:3072],
                                          op=ALU.add), "VC")
                    evac_l = c.cnt["VC"]
                    def rope(src, dstb):
                        e.tensor_tensor(out=t1[:].rearrange("p (r x) -> p r x", r=2),
                                        in0=src[:].rearrange("p (r x) -> p r x", r=2),
                                        in1=ctab[b][:, None, :].broadcast_to([128, 2, 512]),
                                        op=ALU.mult)
                        e.tensor_tensor(out=t2[:, :512], in0=src[:, 512:], in1=stab[b][:],
                                        op=ALU.mult)
                        e.tensor_tensor(out=t2[:, 512:], in0=src[:, :512], in1=stab[b][:],
                                        op=ALU.mult)
                        e.tensor_tensor(out=dstb[:, :512], in0=t1[:, :512], in1=t2[:, :512],
                                        op=ALU.subtract)
                        return e.tensor_tensor(out=dstb[:, 512:], in0=t1[:, 512:],
                                               in1=t2[:, 512:], op=ALU.add)
                    ins = rope(kb, krb)
                    if own:
                        ins = rope(qb, qrb)
                    c.inc(ins, "VC")
                evac_tok = c.cnt["VC"] - 1
                free_tok[i] = c.cnt["VC"]
                rope_tok = c.cnt["VC"]

                @blk.sync
                def _(e):
                    issue(e, i + 2)

                @blk.tensor
                def _(e):
                    c.w(e, "VC", rope_tok)
                    c.w(e, t2_free[0], t2_free[1])
                    for ch in range(8):
                        ins = e.transpose(out=ps_t2[:, ch, :], in_=krb[:, 128 * ch:128 * (ch + 1)],
                                          identity=cx.identb[:])
                    c.inc(ins, "TC")

                @blk.vector
                def _(e):
                    c.w(e, "TC")
                    c.inc(e.tensor_copy(out=kT[:], in_=ps_t2[:]), "VC")
                kT_tok = c.cnt["VC"]
                t2_free = ("VC", kT_tok)

                if own:
                    @blk.tensor
                    def _(e):
                        c.w(e, "VC", kT_tok)
                        for ch in range(8):
                            ins = e.transpose(out=ps_t2[:, ch, :],
                                              in_=qrb[:, 128 * ch:128 * (ch + 1)],
                                              identity=cx.identb[:])
                        c.inc(ins, "TC")

                    @blk.vector
                    def _(e):
                        c.w(e, "TC")
                        c.inc(e.tensor_copy(out=qT[:], in_=ps_t2[:]), "VC")
                    t2_free = ("VC", c.cnt["VC"])

                out_tok = c.cnt["VC"]

                @blk.gpsimd
                def _(e):
                    c.w(e, "VC", out_tok)
                    c.inc(e.dma_start(out=ap["KFM"][:, :, 128 * i:128 * (i + 1)].rearrange("c p t -> p c t"),
                                      in_=kT[:]), "LG", 16)
                    c.inc(e.dma_start(out=ap["VTM"][128 * i:128 * (i + 1), :], in_=vb[:]),
                          "LG", 16)
                    if own:
                        c.inc(e.dma_start(out=ap["QFM"][:, :, 128 * i:128 * (i + 1)].rearrange("c p t -> p c t"),
                                          in_=qT[:]), "LG", 16)
                lg_now = c.cnt["LG"]
                free_tok[i] = max(free_tok[i], rope_tok)
            if i + 1 < NT_S:
                with nc.Block() as blk2:
                    @blk2.vector
                    def _(e):
                        c.w(e, "LG", lg_now)

                    @blk2.scalar
                    def _(e):
                        c.w(e, "LG", lg_now)
                    if _SERIAL:
                        for eng in (blk2.vector, blk2.scalar, blk2.tensor,
                                    blk2.gpsimd, blk2.sync):
                            def _barrier(e):
                                for s in ("LD", "LG", "ST", "TC", "VC", "SC"):
                                    c.w(e, s)
                            eng(_barrier)

        with nc.Block() as blk:
            @blk.sync
            def _(e):
                c.w(e, "LG")
                c.w(e, "ST")



def phaseA(cx):
    nc, ap, c = cx.nc, cx.ap, cx
    with ExitStack() as st:
        def sb(name, shape, dt):
            return st.enter_context(nc.sbuf_tensor(name, shape, dt))

        def psum(name, shape, dt=F32):
            return st.enter_context(nc.psum_tensor(name, shape, dt))

        wp = sb("pa_wp", [128, 8, 1024], BF16)
        bpr = sb("pa_bpr", [128, 1024], F32)
        qf = [sb(f"pa_qf{j}", [128, 8, 128], BF16) for j in range(8)]
        O = [sb(f"pa_O{j}", [128, 1024], F32) for j in range(8)]
        kblk = [sb(f"pa_k{h}", [128, 8, 1024], BF16) for h in range(2)]
        vblk = [sb(f"pa_v{h}", [128, 8, 1024], BF16) for h in range(2)]
        S_sb = sb("pa_S", [128, 2048], F32)
        P_sb = sb("pa_P", [128, 2048], BF16)
        at_sb = sb("pa_at", [128, 16, 128], BF16)
        nm = sb("pa_nm", [128, 8, 2], F32)
        ll = sb("pa_l", [128, 8], F32)
        nmb = sb("pa_nmb", [128, 1], F32)
        dd = sb("pa_dd", [128, 1], F32)
        corr = sb("pa_corr", [128, 1], F32)
        lb = sb("pa_lb", [128, 1], F32)
        linv = sb("pa_linv", [128, 8], F32)
        ob = sb("pa_ob", [128, 1024], BF16)
        oT = sb("pa_oT", [128, 8, 128], BF16)
        xnt = [sb(f"pa_xn{h}", [128, 1024], F32) for h in range(2)]
        stile = [sb(f"pa_s{h}", [128, 1024], F32) for h in range(2)]
        ps_s = [psum(f"pa_pss{h}", [128, 1024]) for h in range(2)]
        ps_at = psum("pa_psat", [128, 16, 128], BF16)
        ps_pv = psum("pa_pspv", [128, 1024])

        with nc.Block() as blk:
            @blk.gpsimd
            def _(e):
                c.w(e, "LG")
                c.inc(e.dma_start(out=wp[:], in_=ap["wproj"].rearrange(
                    "(c p) n -> p c n", p=128)), "LG", 16)

            @blk.sync
            def _(e):
                c.w(e, "ST")
                c.inc(e.dma_start(out=bpr[:], in_=ap["bproj"][None, :].broadcast_to(
                    [128, 1024])), "LD", 16)
        wp_lg, wp_ld = c.cnt["LG"], c.cnt["LD"]

        state = {"pend": None, "p_tok": 0, "at_evac": 0, "pv_done": 0,
                 "sc_evac_v": 0, "sc_evac_s": 0}

        def flush_pending():
            pj = state["pend"]
            if pj is None:
                return
            with nc.Block() as blk:
                @blk.tensor
                def _(e):
                    c.w(e, "SC", state["p_tok"])
                    c.w(e, "VC", state["at_evac"])
                    for ch2 in range(16):
                        ins = e.transpose(out=ps_at[:, ch2, :],
                                          in_=P_sb[:, 128 * ch2:128 * (ch2 + 1)],
                                          identity=cx.identb[:])
                    c.inc(ins, "TC")
                tr_tok = c.cnt["TC"]

                @blk.vector
                def _(e):
                    c.w(e, "TC", tr_tok)
                    c.inc(e.tensor_copy(out=at_sb[:], in_=ps_at[:]), "VC")
                state["at_evac"] = c.cnt["VC"]

                @blk.tensor
                def _(e):
                    c.w(e, "VC", state["at_evac"])
                    c.w(e, "VC", state["pv_done"])
                    for h2 in range(2):
                        for ch2 in range(16):
                            ins = e.matmul(out=ps_pv[:, 512 * h2:512 * (h2 + 1)],
                                           lhsT=at_sb[:, ch2, :],
                                           rhs=vblk[ch2 // 8][:, ch2 % 8,
                                                             512 * h2:512 * (h2 + 1)],
                                           start=(ch2 == 0), stop=(ch2 == 15))
                    c.inc(ins, "TC")
                pvmm = c.cnt["TC"]

                @blk.vector
                def _(e):
                    c.w(e, "TC", pvmm)
                    c.inc(e.tensor_tensor(out=O[pj][:], in0=O[pj][:], in1=ps_pv[:],
                                          op=ALU.add), "VC")
                state["pv_done"] = c.cnt["VC"]
            state["pend"] = None

        kv_free = (0, 0)
        for g in range(4):
            with nc.Block() as blk:
                @blk.sync
                def _(e):
                    if g == 0:
                        c.w(e, "LG", wp_lg)
                    else:
                        c.w(e, "TC")
                    for j in range(8):
                        jj = 8 * g + j
                        c.inc(e.dma_start(out=qf[j][:], in_=ap["QFM"][
                            :, :, 128 * jj:128 * (jj + 1)].rearrange("c p t -> p c t")),
                            "LD", 16)

                @blk.vector
                def _(e):
                    c.w(e, "VC")
                    for j in range(8):
                        e.memset(O[j][:], 0.0)
                    e.memset(nm[:, :, 0:1], 3.0e38)
                    c.inc(e.memset(ll[:], 0.0), "VC")
            qf_tok = c.cnt["LD"]
            init_tok = c.cnt["VC"]

            for sbi in range(4):
                kb0 = 2048 * sbi
                with nc.Block() as blk:
                    @blk.sync
                    def _(e):
                        c.w(e, "TC", kv_free[0])
                        c.w(e, "VC", kv_free[1])
                        for h in range(2):
                            c.inc(e.dma_start(out=kblk[h][:], in_=ap["KFM"][
                                :, :, kb0 + 1024 * h:kb0 + 1024 * (h + 1)].rearrange(
                                "c p t -> p c t")), "LD", 16)
                            c.inc(e.dma_start(out=vblk[h][:], in_=ap["VTM"][
                                kb0 + 1024 * h:kb0 + 1024 * (h + 1), :].rearrange(
                                "(n p) d -> p n d", p=128)), "LD", 16)
                kv_tok = c.cnt["LD"]

                last_sc = 0
                for j in range(8):
                    with nc.Block() as blk:
                        @blk.tensor
                        def _(e):
                            c.w(e, "LD", max(kv_tok, qf_tok))
                            for h in range(2):
                                if h == 0:
                                    c.w(e, "VC", state["sc_evac_v"])
                                else:
                                    c.w(e, "VC", state["sc_evac_s"])
                                for b2 in range(2):
                                    for ch in range(8):
                                        ins = e.matmul(
                                            out=ps_s[h][:, 512 * b2:512 * (b2 + 1)],
                                            lhsT=qf[j][:, ch, :],
                                            rhs=kblk[h][:, ch, 512 * b2:512 * (b2 + 1)],
                                            start=(ch == 0), stop=(ch == 7))
                                c.inc(ins, "TC")
                    sc_tok = c.cnt["TC"]
                    last_sc = sc_tok

                    flush_pending()

                    with nc.Block() as blk:
                        @blk.vector
                        def _(e):
                            c.w(e, "TC", sc_tok - 1)
                            c.w(e, "VC", init_tok)
                            c.w(e, "SC", state["p_tok"])
                            c.inc(e.tensor_copy(out=S_sb[:, :1024], in_=ps_s[0][:]),
                                  "VC")
                        state["sc_evac_v"] = c.cnt["VC"]

                        @blk.vector
                        def _(e):
                            c.w(e, "TC", sc_tok)
                            c.inc(e.tensor_copy(out=S_sb[:, 1024:], in_=ps_s[1][:]),
                                  "VC")
                        state["sc_evac_s"] = c.cnt["VC"]

                        old, new = sbi % 2, 1 - sbi % 2
                        @blk.vector
                        def _(e):
                            c.vs(e, e.reduce_max(out=nmb[:], in_=S_sb[:], axis=AX.X,
                                                 negate=True))
                            c.vs(e, e.tensor_tensor(out=nm[:, j, new:new + 1],
                                                    in0=nm[:, j, old:old + 1],
                                                    in1=nmb[:], op=ALU.min))
                            c.inc(e.tensor_tensor(out=dd[:], in0=nm[:, j, new:new + 1],
                                                  in1=nm[:, j, old:old + 1],
                                                  op=ALU.subtract), "VC")
                        sm_tok = c.cnt["VC"]

                        @blk.scalar
                        def _(e):
                            c.w(e, "VC", sm_tok)
                            c.inc(e.activation(out=corr[:], in_=dd[:], func=AF.Exp),
                                  "SC")
                            c.inc(e.activation(out=P_sb[:], in_=S_sb[:], func=AF.Exp,
                                               bias=nm[:, j, new:new + 1],
                                               accum_out=lb[:]), "SC")
                        state["p_tok"] = c.cnt["SC"]

                        @blk.vector
                        def _(e):
                            c.w(e, "SC", state["p_tok"])
                            c.vs(e, e.tensor_scalar(out=ll[:, j:j + 1],
                                                    in0=ll[:, j:j + 1],
                                                    scalar1=corr[:], scalar2=None,
                                                    op0=ALU.mult))
                            c.vs(e, e.tensor_tensor(out=ll[:, j:j + 1],
                                                    in0=ll[:, j:j + 1],
                                                    in1=lb[:], op=ALU.add))
                            c.inc(e.tensor_scalar(out=O[j][:], in0=O[j][:],
                                                  scalar1=corr[:], scalar2=None,
                                                  op0=ALU.mult), "VC")
                    state["pend"] = j
                flush_pending()
                kv_free = (last_sc, state["pv_done"])

            with nc.Block() as blk:
                @blk.vector
                def _(e):
                    c.w(e, "VC", state["pv_done"])
                    c.inc(e.reciprocal(out=linv[:], in_=ll[:]), "VC")
                    c.w(e, "VC")
            oT_free = 0
            last_stile = 0
            st_tok = [0, 0]
            for j in range(8):
                jj = 8 * g + j
                with nc.Block() as blk:
                    @blk.sync
                    def _(e):
                        c.inc(e.dma_start(out=xnt[j % 2][:],
                                          in_=ap["XN"][128 * jj:128 * (jj + 1), :]),
                              "LD", 16)
                    xn_tok = c.cnt["LD"]

                    @blk.vector
                    def _(e):
                        if j >= 1:
                            c.w(e, "TC")
                        c.inc(e.tensor_scalar(out=ob[:], in0=O[j][:],
                                              scalar1=linv[:, j:j + 1], scalar2=None,
                                              op0=ALU.mult), "VC")
                    ob_tok = c.cnt["VC"]

                    @blk.tensor
                    def _(e):
                        c.w(e, "VC", ob_tok)
                        c.w(e, "VC", oT_free if isinstance(oT_free, int) else 0)
                        for ch in range(8):
                            ins = e.transpose(out=ps_at[:, ch, :],
                                              in_=ob[:, 128 * ch:128 * (ch + 1)],
                                              identity=cx.identb[:])
                        c.inc(ins, "TC")
                    trj = c.cnt["TC"]

                    @blk.vector
                    def _(e):
                        c.w(e, "TC", trj)
                        c.inc(e.tensor_copy(out=oT[:], in_=ps_at[:, :8, :]), "VC")
                    oT_free = c.cnt["VC"]

                    @blk.tensor
                    def _(e):
                        c.w(e, "VC", oT_free)
                        c.w(e, "VC", last_stile)
                        for h in range(2):
                            for ch in range(8):
                                ins = e.matmul(out=ps_s[0][:, 512 * h:512 * (h + 1)],
                                               lhsT=oT[:, ch, :],
                                               rhs=wp[:, ch, 512 * h:512 * (h + 1)],
                                               start=(ch == 0), stop=(ch == 7))
                        c.inc(ins, "TC")
                    mmj = c.cnt["TC"]

                    @blk.vector
                    def _(e):
                        c.w(e, "TC", mmj)
                        c.w(e, "LD", xn_tok)
                        if g == 0 and j == 0:
                            c.w(e, "LD", wp_ld)
                        c.w(e, "ST", st_tok[j % 2])
                        e.tensor_tensor(out=stile[j % 2][:], in0=ps_s[0][:],
                                        in1=bpr[:], op=ALU.add)
                        c.inc(e.tensor_tensor(out=stile[j % 2][:],
                                              in0=stile[j % 2][:],
                                              in1=xnt[j % 2][:], op=ALU.add), "VC")
                    last_stile = c.cnt["VC"]

                    @blk.sync
                    def _(e):
                        c.w(e, "VC", last_stile)
                        c.inc(e.dma_start(out=ap["SOUT"][128 * jj:128 * (jj + 1), :],
                                          in_=stile[j % 2][:]), "ST", 16)
                    st_tok[j % 2] = c.cnt["ST"]
            state["sc_evac_v"] = last_stile
            kv_free = (c.cnt["TC"], state["pv_done"])

        with nc.Block() as blk:
            @blk.sync
            def _(e):
                c.w(e, "ST")


def phaseB(cx, layer):
    nc, ap, c = cx.nc, cx.ap, cx
    AQ0, AK0 = L_AQ0[layer], L_AK0[layer]
    xsrc = ap["XN"] if layer == 0 else ap["X1"]
    xmid = ap["XMID"] if layer == 0 else ap["XMID2"]
    xout = ap["X1"] if layer == 0 else ap["X2"]

    def xrow(tp):
        if layer == 0:
            return 128 * (((tp - PADT) % (NT_S)) if True else tp)
        return 128 * tp

    with ExitStack() as lst:
        w1 = lst.enter_context(nc.sbuf_tensor(f"pb{layer}_w1", [128, 8, 2 * HID], BF16))
        stats = lst.enter_context(nc.sbuf_tensor(f"pb{layer}_stats", [128, NT_L, 2], F32))
        eps5 = lst.enter_context(nc.sbuf_tensor(f"pb{layer}_eps5", [128, 1], F32))

        with ExitStack() as st:
            def sb(name, shape, dt):
                return st.enter_context(nc.sbuf_tensor(f"pb{layer}a_" + name, shape, dt))

            def psum(name, shape, dt=F32):
                return st.enter_context(nc.psum_tensor(f"pb{layer}a_" + name, shape, dt))

            wqkv = sb("wqkv", [128, 8, 3 * INNER], BF16)
            wo = sb("wo", [128, 4, 1024], BF16)
            masks = sb("masks", [128, 5, 640], BF16)
            qsR = sb("qsR", [128, 64], F32)
            ksR = sb("ksR", [128, 64], F32)
            xt = [sb(f"x{i}", [128, 1024], F32) for i in range(2)]
            xp = [sb(f"xp{i}", [128, 128], F32) for i in range(2)]
            xb = sb("xb", [128, 1024], BF16)
            xqT = sb("xqT", [128, 8, 129], BF16)
            qsb = sb("qsb", [128, 512], F32)
            ksb = sb("ksb", [128, 512], F32)
            sqq = sb("sqq", [128, 512], F32)
            ssq = sb("ssq", [128, 8], F32)
            inq = sb("inq", [128, 8], F32)
            ink = sb("ink", [128, 8], F32)
            q2 = sb("q2", [128, 512], F32)
            k2 = sb("k2", [128, 512], F32)
            t1_ = sb("t1", [128, 512], F32)
            u_ = sb("u", [128, 512], F32)
            qxb = sb("qxb", [128, 512], BF16)
            kxb = sb("kxb", [128, 512], BF16)
            qT = sb("qT", [128, 4, 128], BF16)
            kring = sb("kring", [128, 4, 768], BF16)
            vring = sb("vring", [128, 6, 512], BF16)
            Ss = sb("Ss", [128, 640], F32)
            Pb = sb("Pb", [128, 640], BF16)
            at2 = sb("at2", [128, 5, 128], BF16)
            nm8 = sb("nm8", [128, 8], F32)
            lb8 = sb("lb8", [128, 8], F32)
            li8 = sb("li8", [128, 8], F32)
            ob = sb("ob", [128, 512], BF16)
            oT = sb("oT", [128, 4, 128], BF16)
            xm = [sb(f"xm{i}", [128, 1024], F32) for i in range(2)]
            st4 = sb("st4", [128, 4], F32)
            sh2 = sb("sh2", [128, 2], F32)
            pv2 = sb("pv2", [1, 4], F32)
            pv2g = sb("pv2g", [1, 2], F32)
            mtmp = sb("mtmp", [128, 1], F32)
            vtmp = sb("vtmp", [128, 1], F32)
            ps_qkv = psum("qkv", [128, 3, 512])
            ps_t = psum("t", [128, 8, 128], BF16)
            ps_sc = psum("sc", [128, 1024])
            ps_at = psum("at", [128, 5, 128], BF16)
            ps_pv = psum("pv", [128, 512])

            with nc.Block() as blk:
                @blk.gpsimd
                def _(e):
                    c.w(e, "LG")
                    c.inc(e.dma_start(out=wqkv[:], in_=ap["wqkv_b"][layer].rearrange(
                        "(c p) n -> p c n", p=128)), "LG", 16)
                    c.inc(e.dma_start(out=wo[:], in_=ap["wo_b"][layer].rearrange(
                        "(c p) n -> p c n", p=128)), "LG", 16)
                    c.inc(e.dma_start(out=w1[:], in_=ap["w1_b"][layer].rearrange(
                        "(c p) n -> p c n", p=128)), "LG", 16)

                @blk.sync
                def _(e):
                    c.w(e, "ST")
                    c.inc(e.dma_start(out=masks[:], in_=ap["masksB"].rearrange(
                        "m p n -> p m n")), "LD", 16)
                    c.inc(e.dma_start(out=qsR[:], in_=ap["qscale"][layer][None, :]
                                      .broadcast_to([128, 64])), "LD", 16)
                    c.inc(e.dma_start(out=ksR[:], in_=ap["kscale"][layer][None, :]
                                      .broadcast_to([128, 64])), "LD", 16)
                tab_tok = c.cnt["LD"]

                @blk.vector
                def _(e):
                    e.memset(eps5[:], 1e-5)
                    e.memset(pv2[:], 0.0)
                    e.memset(pv2g[:], 0.0)
                    c.w(e, "LD", tab_tok)
                    c.inc(e.tensor_scalar(out=qsR[:], in0=qsR[:], scalar1=8.0,
                                          scalar2=None, op0=ALU.mult), "VC")
            w_tok = c.cnt["LG"]
            pro_tok = c.cnt["VC"]

            ld_tok = {}
            x_free = {}

            def issue(e, tp):
                if tp >= NT_L:
                    return
                if tp - 2 in x_free:
                    c.w(e, "VC", x_free[tp - 2])
                bb = tp % 2
                c.inc(e.dma_start(out=xt[bb][:],
                                  in_=xsrc[xrow(tp):xrow(tp) + 128, :]), "LD", 16)
                c.inc(e.dma_start(out=xp[bb][:],
                                  in_=ap["xposP"][128 * tp:128 * (tp + 1), :]),
                      "LD", 16)
                ld_tok[tp] = c.cnt["LD"]

            toks = {"xqT_evac": 0, "qkv_evac": 0, "ring_evac": 0, "pst_free": 0,
                    "scm": [0, 0], "p_done": 0, "at_done": 0, "pv_evac": 0,
                    "xm_st": [0, 0], "stats_done": 0, "wo_evac": 0, "l2_done": 0}

            for tp in range(AK0, NT_L):
                b = tp % 2
                qtile = tp >= AQ0
                sl = tp % 6
                with nc.Block() as blk:
                    if tp == AK0:
                        @blk.sync
                        def _(e):
                            issue(e, AK0)
                            issue(e, AK0 + 1)

                    @blk.vector
                    def _(e):
                        c.w(e, "LD", ld_tok[tp])
                        c.w(e, "VC", pro_tok)
                        c.inc(e.tensor_copy(out=xb[:], in_=xt[b][:]), "VC")
                        c.w(e, "SC", toks["xqT_evac"])
                        c.w(e, "VC")
                        if tp == AK0:
                            ins = e.memset(xqT[:, :, 0:1], 0.0)
                        elif tp == GT:
                            ins = e.tensor_scalar(out=xqT[:, :, 0:1],
                                                  in0=xqT[:, :, 128:129],
                                                  scalar1=cx.gate[:], scalar2=None,
                                                  op0=ALU.mult)
                        else:
                            ins = e.tensor_copy(out=xqT[:, :, 0:1],
                                                in_=xqT[:, :, 128:129])
                        c.inc(ins, "VC")
                    stage_tok = c.cnt["VC"]

                    @blk.tensor
                    def _(e):
                        c.w(e, "VC", stage_tok)
                        c.w(e, "SC", toks["ring_evac"])
                        c.w(e, "SC", toks["xqT_evac"])
                        for ch in range(8):
                            ins = e.transpose(out=ps_t[:, ch, :],
                                              in_=xb[:, 128 * ch:128 * (ch + 1)],
                                              identity=cx.identb[:])
                        c.inc(ins, "TC")
                    trx_tok = c.cnt["TC"]

                    @blk.scalar
                    def _(e):
                        c.w(e, "TC", trx_tok)
                        c.w(e, "VC", stage_tok)
                        c.inc(e.copy(out=xqT[:, :, 1:129], in_=ps_t[:]), "SC")
                    toks["xqT_evac"] = c.cnt["SC"]

                    @blk.tensor
                    def _(e):
                        c.w(e, "SC", toks["xqT_evac"])
                        if tp == AK0:
                            c.w(e, "LG", w_tok)
                        c.w(e, "VC", toks["qkv_evac"])
                        for oi, c0 in ((0, 0), (1, 512), (2, 1024)):
                            for ch in range(8):
                                lhs = xqT[:, ch, 1:129] if ch < 4 else xqT[:, ch, 0:128]
                                ins = e.matmul(out=ps_qkv[:, oi, :], lhsT=lhs,
                                               rhs=wqkv[:, ch, c0:c0 + 512],
                                               start=(ch == 0), stop=(ch == 7))
                        c.inc(ins, "TC")
                    mm_tok = c.cnt["TC"]

                    @blk.scalar
                    def _(e):
                        c.w(e, "TC", mm_tok)
                        e.copy(out=qsb[:], in_=ps_qkv[:, 0, :])
                        c.inc(e.copy(out=ksb[:], in_=ps_qkv[:, 1, :]), "SC")
                    qk_tok = c.cnt["SC"]

                    @blk.vector
                    def _(e):
                        c.w(e, "TC", mm_tok)
                        c.w(e, "TC")
                        c.inc(e.tensor_copy(out=vring[:, sl, :], in_=ps_qkv[:, 2, :]),
                              "VC")
                    toks["qkv_evac"] = c.cnt["VC"]

                    def norml2(eng_s, eng_v, src_, invout, scaleR, sqtok):
                        pass

                    @blk.scalar
                    def _(e):
                        c.w(e, "SC", qk_tok)
                        c.w(e, "VC", toks["l2_done"])
                        c.inc(e.activation(out=sqq[:], in_=ksb[:], func=AF.Square),
                              "SC")
                    sqk_tok = c.cnt["SC"]

                    @blk.vector
                    def _(e):
                        c.w(e, "SC", sqk_tok)
                        c.vs(e, e.reduce_sum(out=ssq[:, :, None], in_=sqq[:].rearrange(
                            "p (h d) -> p h d", d=64), axis=AX.X))
                        c.inc(e.tensor_scalar(out=ssq[:], in0=ssq[:], scalar1=1e-24,
                                              scalar2=None, op0=ALU.max), "VC")
                    ssk_tok = c.cnt["VC"]

                    @blk.scalar
                    def _(e):
                        c.w(e, "VC", ssk_tok)
                        e.activation(out=sqq[:, :8], in_=ssq[:], func=AF.Ln)
                        c.inc(e.activation(out=ink[:], in_=sqq[:, :8], func=AF.Exp,
                                           scale=-0.5), "SC")
                    invk_tok = c.cnt["SC"]

                    if qtile:
                        @blk.scalar
                        def _(e):
                            c.inc(e.activation(out=sqq[:], in_=qsb[:], func=AF.Square),
                                  "SC")
                        sqq_tok = c.cnt["SC"]

                        @blk.vector
                        def _(e):
                            c.w(e, "SC", sqq_tok)
                            c.vs(e, e.reduce_sum(out=ssq[:, :, None],
                                                 in_=sqq[:].rearrange(
                                                     "p (h d) -> p h d", d=64),
                                                 axis=AX.X))
                            c.inc(e.tensor_scalar(out=ssq[:], in0=ssq[:],
                                                  scalar1=1e-24, scalar2=None,
                                                  op0=ALU.max), "VC")
                        ssqq_tok = c.cnt["VC"]

                        @blk.scalar
                        def _(e):
                            c.w(e, "VC", ssqq_tok)
                            e.activation(out=sqq[:, :8], in_=ssq[:], func=AF.Ln)
                            c.inc(e.activation(out=inq[:], in_=sqq[:, :8],
                                               func=AF.Exp, scale=-0.5), "SC")
                        invq_tok = c.cnt["SC"]

                    def xpos(e, src_, inv_, scR, coff, soff, sign_swap, dstb):
                        e.tensor_tensor(out=q2[:].rearrange("p (h d) -> p h d", d=64),
                                        in0=src_[:].rearrange("p (h d) -> p h d", d=64),
                                        in1=inv_[:, :, None].broadcast_to([128, 8, 64]),
                                        op=ALU.mult)
                        e.tensor_tensor(out=q2[:].rearrange("p (h d) -> p h d", d=64),
                                        in0=q2[:].rearrange("p (h d) -> p h d", d=64),
                                        in1=scR[:, None, :].broadcast_to([128, 8, 64]),
                                        op=ALU.mult)
                        v4 = lambda t: t[:].rearrange("p (h r d) -> p h r d", r=2, d=32)
                        e.tensor_tensor(out=v4(t1_), in0=v4(q2),
                                        in1=xp[b][:, None, None, coff:coff + 32]
                                        .broadcast_to([128, 8, 2, 32]), op=ALU.mult)
                        e.tensor_tensor(out=v4(u_)[:, :, 0, :], in0=v4(q2)[:, :, 1, :],
                                        in1=xp[b][:, None, soff:soff + 32]
                                        .broadcast_to([128, 8, 32]), op=ALU.mult)
                        e.tensor_tensor(out=v4(u_)[:, :, 1, :], in0=v4(q2)[:, :, 0, :],
                                        in1=xp[b][:, None, soff:soff + 32]
                                        .broadcast_to([128, 8, 32]), op=ALU.mult)
                        e.tensor_tensor(out=v4(dstb)[:, :, 0, :], in0=v4(t1_)[:, :, 0, :],
                                        in1=v4(u_)[:, :, 0, :], op=ALU.subtract)
                        return e.tensor_tensor(out=v4(dstb)[:, :, 1, :],
                                               in0=v4(t1_)[:, :, 1, :],
                                               in1=v4(u_)[:, :, 1, :], op=ALU.add)

                    @blk.vector
                    def _(e):
                        c.w(e, "SC", invk_tok)
                        ins = xpos(e, ksb, ink, ksR, 64, 96, True, kxb)
                        c.inc(ins, "VC")
                        if qtile:
                            c.w(e, "SC", invq_tok)
                            ins = xpos(e, qsb, inq, qsR, 0, 32, True, qxb)
                            c.inc(ins, "VC")
                    toks["l2_done"] = c.cnt["VC"]
                    rope_tok = c.cnt["VC"]

                    @blk.tensor
                    def _(e):
                        c.w(e, "VC", rope_tok)
                        c.w(e, "SC", toks["ring_evac"])
                        for ch in range(4):
                            ins = e.transpose(out=ps_t[:, ch, :],
                                              in_=kxb[:, 128 * ch:128 * (ch + 1)],
                                              identity=cx.identb[:])
                        if qtile:
                            for ch in range(4):
                                ins = e.transpose(out=ps_t[:, 4 + ch, :],
                                                  in_=qxb[:, 128 * ch:128 * (ch + 1)],
                                                  identity=cx.identb[:])
                        c.inc(ins, "TC")
                    trk_tok = c.cnt["TC"]

                    @blk.scalar
                    def _(e):
                        c.w(e, "TC", trk_tok)
                        ins = e.copy(out=kring[:, :, 128 * sl:128 * (sl + 1)],
                                     in_=ps_t[:, 0:4, :])
                        if qtile:
                            ins = e.copy(out=qT[:], in_=ps_t[:, 4:8, :])
                        c.inc(ins, "SC")
                    toks["ring_evac"] = c.cnt["SC"]
                    ring_tok = c.cnt["SC"]

                x_free[tp] = rope_tok
                if not qtile:
                    with nc.Block() as blk:
                        @blk.sync
                        def _(e):
                            issue(e, tp + 2)
                    continue

                slots = [(tp - 4 + i) % 6 for i in range(5)]
                runs = []
                for s in slots:
                    if runs and runs[-1][0] + runs[-1][1] == s:
                        runs[-1] = (runs[-1][0], runs[-1][1] + 1)
                    else:
                        runs.append((s, 1))
                mi = tp - GT if (GT <= tp < GT + 4) else 4

                for hh in range(8):
                    p0 = 64 * (hh % 2)
                    chq = hh // 2
                    with nc.Block() as blk:
                        @blk.tensor
                        def _(e):
                            c.w(e, "SC", ring_tok)
                            c.w(e, "VC", toks["scm"][0])
                            for i5 in range(5):
                                ins = e.matmul(
                                    out=ps_sc[:, 128 * i5:128 * (i5 + 1)],
                                    lhsT=qT[p0:p0 + 64, chq, :],
                                    rhs=kring[p0:p0 + 64, chq,
                                              128 * slots[i5]:128 * (slots[i5] + 1)],
                                    start=True, stop=True)
                            c.inc(ins, "TC")
                        sc_tok = c.cnt["TC"]

                        @blk.vector
                        def _(e):
                            c.w(e, "TC", sc_tok)
                            c.w(e, "SC", toks["p_done"])
                            c.inc(e.scalar_tensor_tensor(
                                out=Ss[:], in0=ps_sc[:, :640], scalar=1.0,
                                in1=masks[:, mi, :], op0=ALU.mult, op1=ALU.add),
                                "VC")
                            c.vs(e, e.reduce_max(out=nm8[:, hh:hh + 1], in_=Ss[:],
                                                 axis=AX.X, negate=True))
                        toks["scm"][0] = c.cnt["VC"] - 1
                        msk_tok = c.cnt["VC"]

                        @blk.scalar
                        def _(e):
                            c.w(e, "VC", msk_tok)
                            c.w(e, "TC", toks["at_done"])
                            c.inc(e.activation(out=Pb[:], in_=Ss[:], func=AF.Exp,
                                               bias=nm8[:, hh:hh + 1],
                                               accum_out=lb8[:, hh:hh + 1]), "SC")
                        toks["p_done"] = c.cnt["SC"]

                        @blk.tensor
                        def _(e):
                            c.w(e, "SC", toks["p_done"])
                            c.w(e, "SC", toks["at_evac"] if "at_evac" in toks else 0)
                            for i5 in range(5):
                                ins = e.transpose(out=ps_at[:, i5, :],
                                                  in_=Pb[:, 128 * i5:128 * (i5 + 1)],
                                                  identity=cx.identb[:])
                            c.inc(ins, "TC")
                        toks["at_done"] = c.cnt["TC"]

                        @blk.scalar
                        def _(e):
                            c.w(e, "TC", toks["at_done"])
                            c.inc(e.copy(out=at2[:], in_=ps_at[:]), "SC")
                        toks["at_evac"] = c.cnt["SC"]

                        @blk.tensor
                        def _(e):
                            c.w(e, "SC", toks["at_evac"])
                            if hh == 0:
                                c.w(e, "VC", toks["pv_evac"])
                            for i5 in range(5):
                                ins = e.matmul(out=ps_pv[:, 64 * hh:64 * (hh + 1)],
                                               lhsT=at2[:, i5, :],
                                               rhs=vring[:, slots[i5],
                                                         64 * hh:64 * (hh + 1)],
                                               start=(i5 == 0), stop=(i5 == 4))
                            c.inc(ins, "TC")
                pv_tok = c.cnt["TC"]

                with nc.Block() as blk:
                    @blk.vector
                    def _(e):
                        c.w(e, "TC", pv_tok)
                        c.vs(e, e.reciprocal(out=li8[:], in_=lb8[:]))
                        c.inc(e.tensor_tensor(
                            out=ob[:].rearrange("p (h d) -> p h d", d=64),
                            in0=ps_pv[:].rearrange("p (h d) -> p h d", d=64),
                            in1=li8[:, :, None].broadcast_to([128, 8, 64]),
                            op=ALU.mult), "VC")
                    toks["pv_evac"] = c.cnt["VC"]
                    ob_tok = c.cnt["VC"]

                    @blk.tensor
                    def _(e):
                        c.w(e, "VC", ob_tok)
                        c.w(e, "SC", toks["ring_evac"])
                        for ch in range(4):
                            ins = e.transpose(out=ps_t[:, ch, :],
                                              in_=ob[:, 128 * ch:128 * (ch + 1)],
                                              identity=cx.identb[:])
                        c.inc(ins, "TC")
                    tro_tok = c.cnt["TC"]

                    @blk.scalar
                    def _(e):
                        c.w(e, "TC", tro_tok)
                        c.inc(e.copy(out=oT[:], in_=ps_t[:, 0:4, :]), "SC")
                    oT_tok = c.cnt["SC"]
                    toks["ring_evac"] = oT_tok

                    @blk.tensor
                    def _(e):
                        c.w(e, "SC", oT_tok)
                        c.w(e, "VC", toks["wo_evac"])
                        for h2 in range(2):
                            for ch in range(4):
                                ins = e.matmul(out=ps_sc[:, 512 * h2:512 * (h2 + 1)],
                                               lhsT=oT[:, ch, :],
                                               rhs=wo[:, ch, 512 * h2:512 * (h2 + 1)],
                                               start=(ch == 0), stop=(ch == 3))
                        c.inc(ins, "TC")
                    womm_tok = c.cnt["TC"]

                    @blk.vector
                    def _(e):
                        c.w(e, "TC", womm_tok)
                        c.w(e, "ST", toks["xm_st"][b])
                        c.inc(e.tensor_tensor(out=xm[b][:], in0=ps_sc[:],
                                              in1=xt[b][:], op=ALU.add), "VC")
                    toks["wo_evac"] = c.cnt["VC"]
                    toks["scm"][0] = max(toks["scm"][0], c.cnt["VC"])
                    xm_tok = c.cnt["VC"]

                    @blk.sync
                    def _(e):
                        c.w(e, "VC", xm_tok)
                        c.inc(e.dma_start(out=xmid[128 * tp:128 * (tp + 1), :],
                                          in_=xm[b][:]), "ST", 16)
                    toks["xm_st"][b] = c.cnt["ST"]
                    x_free[tp] = xm_tok

                    @blk.vector
                    def _(e):
                        c.w(e, "VC", xm_tok)
                        c.w(e, "VC", toks["stats_done"])
                        e.reduce_sum(out=st4[:, 2:3], in_=xm[b][:, :512], axis=AX.X)
                        c.vs(e, e.reduce_sum(out=st4[:, 0:1], in_=xm[b][:, 512:],
                                             axis=AX.X))
                    s12_tok = c.cnt["VC"]

                    @blk.scalar
                    def _(e):
                        c.w(e, "VC", xm_tok)
                        e.activation(out=sqq[:], in_=xm[b][:, :512], func=AF.Square,
                                     accum_out=st4[:, 3:4])
                        c.inc(e.activation(out=sqq[:], in_=xm[b][:, 512:],
                                           func=AF.Square, accum_out=st4[:, 1:2]),
                              "SC")
                    q12_tok = c.cnt["SC"]

                    @blk.tensor
                    def _(e):
                        c.w(e, "VC", s12_tok)
                        c.w(e, "SC", q12_tok)
                        c.inc(e.matmul(out=ps_pv[:, 0:2], lhsT=cx.shiftm[:],
                                       rhs=st4[:, 0:2], start=True, stop=True), "TC")
                    shm_tok = c.cnt["TC"]

                    @blk.gpsimd
                    def _(e):
                        c.w(e, "VC", s12_tok)
                        c.w(e, "SC", q12_tok)
                        c.inc(e.dma_start(out=pv2[0:1, 2 * (tp % 2):2 * (tp % 2) + 2],
                                          in_=st4[127:128, 0:2],
                                          allow_slow_non_contiguous=True), "LG", 16)
                    pv2_tok = c.cnt["LG"]
                    toks["pv2_tok"] = pv2_tok

                    @blk.vector
                    def _(e):
                        c.w(e, "TC", shm_tok)
                        c.w(e, "LG", toks.get("pv2_prev", 0))
                        c.vs(e, e.tensor_copy(out=sh2[:], in_=ps_pv[:, 0:2]))
                        ps_ = 2 * ((tp - 1) % 2)
                        prev_ap = pv2[0:1, ps_:ps_ + 2]
                        if tp == AQ0:
                            prev_ap = pv2g[:]
                        elif tp == GT:
                            c.vs(e, e.tensor_scalar(out=pv2g[:], in0=prev_ap,
                                                    scalar1=cx.gate[0:1, :],
                                                    scalar2=None, op0=ALU.mult))
                            prev_ap = pv2g[:]
                        c.vs(e, e.tensor_tensor(out=sh2[0:1, :], in0=sh2[0:1, :],
                                                in1=prev_ap, op=ALU.add))
                        c.vs(e, e.tensor_tensor(out=mtmp[:], in0=st4[:, 2:3],
                                                in1=sh2[:, 0:1], op=ALU.add))
                        c.vs(e, e.tensor_scalar(out=stats[:, tp, 0:1], in0=mtmp[:],
                                                scalar1=1.0 / 1024, scalar2=None,
                                                op0=ALU.mult))
                        c.vs(e, e.tensor_tensor(out=vtmp[:], in0=st4[:, 3:4],
                                                in1=sh2[:, 1:2], op=ALU.add))
                        c.vs(e, e.tensor_scalar(out=vtmp[:], in0=vtmp[:],
                                                scalar1=1.0 / 1024, scalar2=None,
                                                op0=ALU.mult))
                        c.vs(e, e.tensor_tensor(out=mtmp[:], in0=stats[:, tp, 0:1],
                                                in1=stats[:, tp, 0:1], op=ALU.mult))
                        c.inc(e.tensor_tensor(out=vtmp[:], in0=vtmp[:], in1=mtmp[:],
                                              op=ALU.subtract), "VC")
                    var2_tok = c.cnt["VC"]
                    toks["pv_evac"] = max(toks["pv_evac"], c.cnt["VC"])

                    @blk.scalar
                    def _(e):
                        c.w(e, "VC", var2_tok)
                        e.activation(out=vtmp[:], in_=vtmp[:], func=AF.Ln,
                                     bias=eps5[:])
                        c.inc(e.activation(out=stats[:, tp, 1:2], in_=vtmp[:],
                                           func=AF.Exp, scale=-0.5), "SC")
                    toks["stats_done"] = c.cnt["VC"]
                    toks["pv2_prev"] = toks["pv2_tok"]
                    stats_sc = c.cnt["SC"]
                with nc.Block() as blk:
                    @blk.sync
                    def _(e):
                        issue(e, tp + 2)

            with nc.Block() as blk:
                @blk.sync
                def _(e):
                    c.w(e, "ST")
                    c.w(e, "LG")

        with ExitStack() as st:
            def sb(name, shape, dt):
                return st.enter_context(nc.sbuf_tensor(f"pb{layer}f_" + name, shape, dt))

            def psum(name, shape, dt=F32):
                return st.enter_context(nc.psum_tensor(f"pb{layer}f_" + name, shape, dt))

            w2 = sb("w2", [128, 22, 1024], BF16)
            B1R = sb("B1R", [128, 2 * HID], F32)
            B2R = sb("B2R", [128, 1024], F32)
            lgR = sb("lgR", [128, 1024], F32)
            lbR = sb("lbR", [128, 1024], F32)
            xmt = [sb(f"xm{i}", [128, 1024], F32) for i in range(2)]
            xmb = sb("xmb", [128, 1024], BF16)
            xmT = sb("xmT", [128, 8, 129], BF16)
            x_sh = sb("xsh", [128, 1024], BF16)
            xhb = xmb
            xhT = sb("xhT", [128, 8, 128], BF16)
            as_ = sb("as", [128, 455], F32)
            gs = sb("gs", [128, 455], F32)
            xh_buf = sb("xh", [128, 1024], F32)
            gel = gs
            h_bf = sb("hbf", [128, HID], BF16)
            hT = sb("hT", [128, 22, 128], BF16)
            x2t = [sb(f"x2{i}", [128, 1024], F32) for i in range(2)]
            ps_t = psum("t", [128, 8, 128], BF16)
            ps_w1 = psum("psw1", [128, 2, 512])
            ps_hT = psum("pshT", [128, 22, 128], BF16)
            ps_w2 = psum("psw2", [128, 1024])

            with nc.Block() as blk:
                @blk.gpsimd
                def _(e):
                    c.w(e, "LG")
                    c.inc(e.dma_start(out=w2[:, 0:21, :], in_=ap["w2_b"][layer][0:2688]
                                      .rearrange("(c p) n -> p c n", p=128)), "LG", 16)
                    c.inc(e.dma_start(out=w2[0:42, 21, :],
                                      in_=ap["w2_b"][layer][2688:2730]), "LG", 16)

                @blk.sync
                def _(e):
                    c.w(e, "ST")
                    c.inc(e.dma_start(out=B1R[:], in_=ap["b1_b"][layer][None, :]
                                      .broadcast_to([128, 2 * HID])), "LD", 16)
                    c.inc(e.dma_start(out=B2R[:], in_=ap["b2_b"][layer][None, :]
                                      .broadcast_to([128, 1024])), "LD", 16)
                    c.inc(e.dma_start(out=lgR[:], in_=ap["lng"][layer][None, :]
                                      .broadcast_to([128, 1024])), "LD", 16)
                    c.inc(e.dma_start(out=lbR[:], in_=ap["lnb"][layer][None, :]
                                      .broadcast_to([128, 1024])), "LD", 16)
            wf_lg = c.cnt["LG"]
            wf_ld = c.cnt["LD"]

            ld_tok = {}
            xm_free = {}

            def issue2(e, tp):
                if tp >= NT_L:
                    return
                if tp - 2 in xm_free:
                    c.w(e, "VC", xm_free[tp - 2])
                c.inc(e.dma_start(out=xmt[tp % 2][:],
                                  in_=xmid[128 * tp:128 * (tp + 1), :]), "LD", 16)
                ld_tok[tp] = c.cnt["LD"]

            t2 = {"xmT_evac": 0, "xsh_evac": 0, "xhT_evac": 0, "w1_evac": 0,
                  "gel_done": 0, "h_cast": 0, "hT_evac": 0, "w2_evac": 0,
                  "x2_st": [0, 0]}

            for tp in range(AQ0, NT_L):
                b = tp % 2
                with nc.Block() as blk:
                    if tp == AQ0:
                        @blk.sync
                        def _(e):
                            issue2(e, AQ0)
                            issue2(e, AQ0 + 1)

                    @blk.vector
                    def _(e):
                        c.w(e, "LD", ld_tok[tp])
                        c.w(e, "TC", t2.get("tr3_prev", 0))
                        c.inc(e.tensor_copy(out=xmb[:], in_=xmt[b][:]), "VC")
                        c.w(e, "VC")
                        if tp == AQ0:
                            ins = e.memset(xmT[:, :, 0:1], 0.0)
                        elif tp == GT:
                            ins = e.tensor_scalar(out=xmT[:, :, 0:1],
                                                  in0=xmT[:, :, 128:129],
                                                  scalar1=cx.gate[:], scalar2=None,
                                                  op0=ALU.mult)
                        else:
                            ins = e.tensor_copy(out=xmT[:, :, 0:1],
                                                in_=xmT[:, :, 128:129])
                        c.inc(ins, "VC")
                    stage_tok = c.cnt["VC"]

                    @blk.tensor
                    def _(e):
                        c.w(e, "VC", stage_tok)
                        c.w(e, "VC", t2["xmT_evac"])
                        for ch in range(8):
                            ins = e.transpose(out=ps_t[:, ch, :],
                                              in_=xmb[:, 128 * ch:128 * (ch + 1)],
                                              identity=cx.identb[:])
                        c.inc(ins, "TC")
                    tr1_tok = c.cnt["TC"]

                    @blk.vector
                    def _(e):
                        c.w(e, "TC", tr1_tok)
                        c.inc(e.tensor_copy(out=xmT[:, :, 1:129], in_=ps_t[:]), "VC")
                    t2["xmT_evac"] = c.cnt["VC"]

                    @blk.tensor
                    def _(e):
                        c.w(e, "VC", t2["xmT_evac"])
                        c.w(e, "VC", t2["xsh_evac"])
                        for ch in range(8):
                            src_ = xmT[:, ch, 1:129] if ch < 4 else xmT[:, ch, 0:128]
                            ins = e.transpose(out=ps_t[:, ch, :], in_=src_,
                                              identity=cx.identb[:])
                        c.inc(ins, "TC")
                    tr2_tok = c.cnt["TC"]

                    @blk.vector
                    def _(e):
                        c.w(e, "TC", tr2_tok)
                        c.inc(e.tensor_copy(out=x_sh[:], in_=ps_t[:]), "VC")
                    t2["xsh_evac"] = c.cnt["VC"]
                    xsh_tok = c.cnt["VC"]

                    @blk.vector
                    def _(e):
                        if tp == AQ0:
                            c.w(e, "LD", wf_ld)
                        c.w(e, "TC", t2.get("tr3_prev", 0))
                        xh = xh_buf[:]
                        e.tensor_scalar(out=xh, in0=x_sh[:],
                                        scalar1=stats[:, tp, 0:1],
                                        scalar2=stats[:, tp, 1:2],
                                        op0=ALU.subtract, op1=ALU.mult)
                        e.tensor_tensor(out=xh, in0=xh, in1=lgR[:], op=ALU.mult)
                        c.inc(e.tensor_tensor(out=xhb[:], in0=xh, in1=lbR[:],
                                              op=ALU.add), "VC")
                    xh_tok = c.cnt["VC"]

                    @blk.tensor
                    def _(e):
                        c.w(e, "VC", xh_tok)
                        c.w(e, "VC", t2["xhT_evac"])
                        for ch in range(8):
                            ins = e.transpose(out=ps_t[:, ch, :],
                                              in_=xhb[:, 128 * ch:128 * (ch + 1)],
                                              identity=cx.identb[:])
                        c.inc(ins, "TC")
                    t2["tr3_prev"] = c.cnt["TC"]
                    tr3_tok = c.cnt["TC"]

                    @blk.vector
                    def _(e):
                        c.w(e, "TC", tr3_tok)
                        c.inc(e.tensor_copy(out=xhT[:], in_=ps_t[:]), "VC")
                    t2["xhT_evac"] = c.cnt["VC"]
                    xhT_tok = c.cnt["VC"]

                    sc0g = c.cnt["SC"]
                    gel_tok = [sc0g + p + 1 for p in range(6)]
                    ag_toks = []
                    for p in range(6):
                        @blk.tensor
                        def _(e):
                            c.w(e, "VC", xhT_tok)
                            if tp == AQ0 and p == 0:
                                c.w(e, "LG", wf_lg)
                            c.w(e, "VC", t2["w1_evac"])
                            for ch in range(8):
                                ins = e.matmul(out=ps_w1[:, 0, :455],
                                               lhsT=xhT[:, ch, :],
                                               rhs=w1[:, ch, 455 * p:455 * (p + 1)],
                                               start=(ch == 0), stop=(ch == 7))
                            for ch in range(8):
                                ins = e.matmul(
                                    out=ps_w1[:, 1, :455], lhsT=xhT[:, ch, :],
                                    rhs=w1[:, ch, HID + 455 * p:HID + 455 * (p + 1)],
                                    start=(ch == 0), stop=(ch == 7))
                            c.inc(ins, "TC")
                        w1mm = c.cnt["TC"]

                        @blk.vector
                        def _(e):
                            c.w(e, "TC", w1mm)
                            if p > 0:
                                c.w(e, "SC", gel_tok[p - 1])
                            e.tensor_tensor(out=gs[:], in0=ps_w1[:, 1, :455],
                                            in1=B1R[:, HID + 455 * p:HID + 455 * (p + 1)],
                                            op=ALU.add)
                            c.inc(e.tensor_tensor(out=as_[:], in0=ps_w1[:, 0, :455],
                                                  in1=B1R[:, 455 * p:455 * (p + 1)],
                                                  op=ALU.add), "VC")
                        ag_toks.append(c.cnt["VC"])
                        t2["w1_evac"] = c.cnt["VC"]

                        @blk.vector
                        def _(e):
                            c.w(e, "SC", gel_tok[p])
                            if p == 0:
                                c.w(e, "TC", t2.get("tr4_prev", 0))
                            c.inc(e.tensor_tensor(out=h_bf[:, 455 * p:455 * (p + 1)],
                                                  in0=as_[:], in1=gel[:],
                                                  op=ALU.mult), "VC")
                    hb_tok = c.cnt["VC"]

                    @blk.scalar
                    def _(e):
                        for p in range(6):
                            c.w(e, "VC", ag_toks[p])
                            c.inc(e.activation(out=gs[:], in_=gs[:],
                                               func=AF.Gelu), "SC")
                    assert c.cnt["SC"] == gel_tok[5]

                    @blk.tensor
                    def _(e):
                        c.w(e, "VC", hb_tok)
                        c.w(e, "VC", t2["hT_evac"])
                        for ch in range(22):
                            wdt = 128 if ch < 21 else HID - 128 * 21
                            ins = e.transpose(out=ps_hT[:wdt, ch, :],
                                              in_=h_bf[:, 128 * ch:128 * ch + wdt],
                                              identity=cx.identb[:])
                        c.inc(ins, "TC")
                    t2["tr4_prev"] = c.cnt["TC"]
                    tr4_tok = c.cnt["TC"]

                    @blk.vector
                    def _(e):
                        c.w(e, "TC", tr4_tok)
                        c.inc(e.tensor_copy(out=hT[:], in_=ps_hT[:]), "VC")
                    t2["hT_evac"] = c.cnt["VC"]
                    hT_tok = c.cnt["VC"]

                    @blk.tensor
                    def _(e):
                        c.w(e, "VC", hT_tok)
                        c.w(e, "VC", t2["w2_evac"])
                        for h2 in range(2):
                            for ch in range(22):
                                wdt = 128 if ch < 21 else HID - 128 * 21
                                ins = e.matmul(out=ps_w2[:, 512 * h2:512 * (h2 + 1)],
                                               lhsT=hT[:wdt, ch, :],
                                               rhs=w2[:wdt, ch, 512 * h2:512 * (h2 + 1)],
                                               start=(ch == 0), stop=(ch == 21))
                        c.inc(ins, "TC")
                    w2mm = c.cnt["TC"]

                    @blk.vector
                    def _(e):
                        c.w(e, "TC", w2mm)
                        c.w(e, "ST", t2["x2_st"][b])
                        e.tensor_tensor(out=x2t[b][:], in0=ps_w2[:], in1=B2R[:],
                                        op=ALU.add)
                        c.inc(e.tensor_tensor(out=x2t[b][:], in0=x2t[b][:],
                                              in1=xmt[b][:], op=ALU.add), "VC")
                    t2["w2_evac"] = c.cnt["VC"]
                    x2_tok = c.cnt["VC"]
                    xm_free[tp] = x2_tok

                    @blk.sync
                    def _(e):
                        c.w(e, "VC", x2_tok)
                        c.inc(e.dma_start(out=xout[128 * tp:128 * (tp + 1), :],
                                          in_=x2t[b][:]), "ST", 16)
                        issue2(e, tp + 2)
                    t2["x2_st"][b] = c.cnt["ST"]

            with nc.Block() as blk:
                @blk.sync
                def _(e):
                    c.w(e, "ST")
                    c.w(e, "LG")


def phaseF(cx):
    nc, ap, c = cx.nc, cx.ap, cx
    with ExitStack() as st:
        def sb(name, shape, dt):
            return st.enter_context(nc.sbuf_tensor("pf_" + name, shape, dt))

        x0t = [sb(f"x0{i}", [128, 1024], F32) for i in range(2)]
        s_t = [sb(f"s{i}", [128, 1024], F32) for i in range(2)]
        x2t = [sb(f"x2{i}", [128, 1024], F32) for i in range(2)]
        g_t = sb("g", [128, 1024], F32)
        o_t = [sb(f"o{i}", [128, 1024], F32) for i in range(2)]

        ld_tok = {}
        done = {}
        st_tok = [0, 0]

        def issue(e, i):
            if i >= NT_H:
                return
            if i - 2 in done:
                c.w(e, "VC", done[i - 2])
            bb = i % 2
            c.inc(e.dma_start(out=x0t[bb][:],
                              in_=ap["x0r"][128 * i:128 * (i + 1), :]), "LD", 16)
            c.inc(e.dma_start(out=s_t[bb][:],
                              in_=ap["SOUT"][128 * i:128 * (i + 1), :]), "LD", 16)
            c.inc(e.dma_start(out=x2t[bb][:],
                              in_=ap["X2"][128 * (PADT + i):128 * (PADT + i + 1), :]),
                  "LD", 16)
            ld_tok[i] = c.cnt["LD"]

        for i in range(NT_H):
            b = i % 2
            with nc.Block() as blk:
                if i == 0:
                    @blk.sync
                    def _(e):
                        c.w(e, "ST")
                        issue(e, 0)
                        issue(e, 1)

                @blk.scalar
                def _(e):
                    c.w(e, "LD", ld_tok[i])
                    c.w(e, "VC", done.get(i - 1, 0))
                    c.inc(e.activation(out=g_t[:], in_=x2t[b][:], func=AF.Silu),
                          "SC")
                g_tok = c.cnt["SC"]

                @blk.vector
                def _(e):
                    c.w(e, "SC", g_tok)
                    c.w(e, "ST", st_tok[b])
                    e.tensor_tensor(out=o_t[b][:], in0=s_t[b][:], in1=g_t[:],
                                    op=ALU.mult)
                    c.inc(e.tensor_tensor(out=o_t[b][:], in0=o_t[b][:],
                                          in1=x0t[b][:], op=ALU.add), "VC")
                done[i] = c.cnt["VC"]

                @blk.sync
                def _(e):
                    c.w(e, "VC", done[i])
                    c.inc(e.dma_start(out=ap["out"][128 * i:128 * (i + 1), :],
                                      in_=o_t[b][:]), "ST", 16)
                    issue(e, i + 2)
                st_tok[b] = c.cnt["ST"]


def kernel(**inputs):
    tabs = _tables()
    nc = build_nc(dbg=_DBG)
    x0 = np.asarray(inputs["L0_attn_x"])
    shared = {
        "wqkv_a": np.asarray(inputs["L0_attn_w_qkv"]),
        "bqkv_a": np.asarray(inputs["L0_attn_b_qkv"]),
        "gnorm": np.asarray(inputs["L0_attn_g_norm"]),
        "wproj": np.asarray(inputs["L0_attn_w_proj"]),
        "bproj": np.asarray(inputs["L0_attn_b_proj"]),
        "wqkv_b": np.asarray(inputs["L0_ffn_wqkv"]),
        "qscale": np.asarray(inputs["L0_ffn_q_scale"]),
        "kscale": np.asarray(inputs["L0_ffn_k_scale"]),
        "wo_b": np.asarray(inputs["L0_ffn_wo"]),
        "lng": np.asarray(inputs["L0_ffn_ln_g"]),
        "lnb": np.asarray(inputs["L0_ffn_ln_b"]),
        "w1_b": np.asarray(inputs["L0_ffn_w1"]),
        "b1_b": np.asarray(inputs["L0_ffn_b1"]),
        "w2_b": np.asarray(inputs["L0_ffn_w2"]),
        "b2_b": np.asarray(inputs["L0_ffn_b2"]),
    }
    in_maps = []
    for core in range(8):
        b, h = core // 2, core % 2
        m = dict(shared)
        m["x0r"] = np.ascontiguousarray(np.roll(x0[b], -HALF * h, axis=0))
        m.update(_tables()[h])
        in_maps.append(m)
    res = run_bass_kernel_spmd(nc, in_maps, core_ids=list(range(8)))
    out = np.empty((BB, S, D), dtype=np.float32)
    for core in range(8):
        b, h = core // 2, core % 2
        if "out" in res.results[core]:
            out[b, HALF * h:HALF * (h + 1)] = np.asarray(res.results[core]["out"])
    kernel._last_results = res.results
    return out
